# version 08
# baseline (speedup 1.0000x reference)
import numpy as np
import concourse.bass as bass
import concourse.mybir as mybir
from concourse.bass_utils import run_bass_kernel_spmd

F32 = mybir.dt.float32
BF16 = mybir.dt.bfloat16
ALU = mybir.AluOpType
AF = mybir.ActivationFunctionType
AX = mybir.AxisListType

P = 128
D = 1024
FC = D // P
DI = 2048
IC = DI // P
HID = 4096
NE = 8
DS = 16
DTR = 64
L = 4096
NSEQ = 2
CH = 512
FCH = 1024
NCH = L // CH
NSC = L // FCH
TS = 256
NSUB = CH // TS
EPS = 1e-5
NCORES = 8

ENGINES = ("sync", "tensor", "vector", "scalar", "gpsimd")


def ts_(i, n):
    return slice(i * n, (i + 1) * n)


class Sem:
    __slots__ = ("h", "n", "name")

    def __init__(self, h, name):
        self.h = h
        self.n = 0
        self.name = name


class Rot:
    def __init__(self, banks):
        self.banks = banks
        self.i = 0
        self.freed = []

    def get(self):
        j = self.i % len(self.banks)
        w = []
        if self.i >= len(self.banks):
            w.append(self.freed[self.i - len(self.banks)])
        self.i += 1
        return self.banks[j], w

    def free(self, pair):
        self.freed.append(pair)


class MK:
    def __init__(self):
        self.nc = bass.Bass("TRN2", target_bir_lowering=False)
        self.prog = {e: [] for e in ENGINES}
        self.pending = {e: [] for e in ENGINES}
        self.floor = {e: {} for e in ENGINES}
        self._soff = (int(self.nc.sbuf_base) + 31) // 32 * 32
        self._ctxs = []
        self.sems = []

    def salloc(self, name, shape, dtype, at=None):
        itm = 2 if dtype == BF16 else 4
        nb = int(np.prod(shape[1:])) * itm
        off = self._soff if at is None else at
        t = self.nc.alloc_sbuf_tensor_at(name, list(shape), dtype, offset=off)
        if at is None:
            self._soff = off + ((nb + 31) // 32) * 32
        return t.ap()

    def sem(self, name):
        cm = self.nc.semaphore(name)
        h = cm.__enter__()
        self._ctxs.append(cm)
        s = Sem(h, name)
        self.sems.append(s)
        return s

    def op(self, eng, build, waits=(), inc=None, k=1):
        fl = self.floor[eng]
        best = {}
        for x in list(self.pending[eng]) + list(waits):
            if x is None or x[1] <= 0:
                continue
            s, v = x
            if v > best.get(s, 0):
                best[s] = v
        if self.pending[eng]:
            self.pending[eng] = []
        w = []
        for s, v in best.items():
            if v > fl.get(s, 0):
                fl[s] = v
                w.append((s, v))
        self.prog[eng].append((w, build, inc.h if inc is not None else None, k))
        if inc is not None:
            inc.n += k

    def dma(self, q, out, in_, waits=(), inc=None, accum=False):
        if accum:
            def b(e, out=out, in_=in_):
                return e.dma_start(out=out, in_=in_, accum_op=ALU.add)
        else:
            def b(e, out=out, in_=in_):
                return e.dma_start(out=out, in_=in_)
        self.op(q, b, waits, inc, 16)

    def fence(self, engines=ENGINES):
        snap = [(s, s.n) for s in self.sems if s.n > 0]
        for e in engines:
            self.pending[e] = list(snap)

    def emit(self):
        block = self.nc.Block()
        block.__enter__()
        self._ctxs.append(block)
        for e in ENGINES:
            lst = self.prog[e]

            def body(eng, lst=lst):
                for w, build, inch, k in lst:
                    for s, v in w:
                        eng.wait_ge(s.h, v)
                    ins = build(eng)
                    if inch is not None:
                        ins.then_inc(inch, k)

            getattr(block, e)(body)


class Stream:
    def __init__(self, mk, q, slots, name):
        self.mk = mk
        self.q = q
        self.slots = slots
        self.sp = mk.sem("stp_" + name)
        self.pi = 0
        self.done_pairs = []

    def put(self, in_ap, extra=()):
        j = self.pi % len(self.slots)
        w = list(extra)
        if self.pi >= len(self.slots):
            w.append(self.done_pairs[self.pi - len(self.slots)])
        self.mk.dma(self.q, self.slots[j], in_ap, waits=w, inc=self.sp)
        self.pi += 1
        return self.slots[j], (self.sp, 16 * self.pi)

    def done(self, pair):
        self.done_pairs.append(pair)


def build_program(dbg=False):
    mk = MK()
    nc = mk.nc

    def din(name, shape, dtype=F32):
        return nc.dram_tensor(name, list(shape), dtype, kind="ExternalInput").ap()

    def dint(name, shape, dtype):
        return nc.dram_tensor(name, list(shape), dtype, kind="Internal").ap()

    x_in = din("x", [NSEQ, D, L])
    wgT = din("wgT", [D, NE])
    wiT = din("wiT", [NE, D, HID])
    woT = din("woT", [NE, HID, D])
    inwT = din("inwT", [2, D, 2 * DI])
    xprojT = din("xprojT", [2, DI, 96])
    dtwT = din("dtwT", [2, DTR, DI])
    moutT = din("moutT", [2, DI, D])
    outwT = din("outwT", [2 * D, D])
    alog = din("alog", [2, P, IC * DS])
    d_in = din("d_r", [2, P, IC])
    dtb_in = din("dtb_r", [2, P, IC])
    cb_in = din("cb_r", [2, P, IC])
    cw_in = din("cw_r", [2, P, IC * 4])
    outb_in = din("outb_r", [P, FC])
    gam_in = din("gam_r", [P, FC])
    bet_in = din("bet_r", [P, FC])
    ident_in = din("ident", [P, P])
    ones1_in = din("ones1", [1, P])
    onesk_in = din("onesk", [P, 1])
    const_in = din("constv", [P, 8])

    out_d = nc.dram_tensor("out", [NSEQ, D, L], F32, kind="ExternalOutput").ap()

    wiB = dint("wiB", [NE, D, HID], BF16)
    woB = dint("woB", [NE, HID, D], BF16)
    inwB = dint("inwB", [2, D, 2 * DI], BF16)
    xprojB = dint("xprojB", [2, DI, 96], BF16)
    dtwB = dint("dtwB", [2, DTR, DI], BF16)
    moutB = dint("moutB", [2, DI, D], BF16)
    outwB = dint("outwB", [2 * D, D], BF16)

    dkind = "ExternalOutput" if dbg else "Internal"
    x1S = nc.dram_tensor("x1S", [NSEQ, D, L], F32, kind=dkind).ap()
    sS = nc.dram_tensor("sS", [NSEQ, D, L], F32, kind=dkind).ap()
    hS = nc.dram_tensor("hS", [NSEQ, D, L], BF16, kind=dkind).ap()
    bcS = dint("bcS", [32, CH], BF16)
    gS = dint("gS", [NE, FCH], BF16)

    A = mk.salloc
    identf = A("identf", [P, P], F32)
    constf = A("constf", [P, 8], F32)
    for _ci, _cv in enumerate((EPS, -0.5, 1.0, -1.0, 0.0, 0.5)):
        nc.const_aps.register(F32, _cv, constf[:, _ci:_ci + 1])
    ones1 = A("ones1s", [1, P], F32)
    oneskb = A("oneskb", [P, 16], BF16)
    wg_sb = A("wg_sb", [P, FC * NE], BF16)
    gam = A("gam", [P, FC], F32)
    bet = A("bet", [P, FC], F32)
    outb = A("outb", [P, FC], F32)
    Dv = A("Dv", [P, IC], F32)
    dtb = A("dtb", [P, IC], F32)
    cbv = A("cbv", [P, IC], F32)
    cwv = A("cwv", [P, IC * 4], F32)
    Aneg = A("Aneg", [P, IC * DS], F32)
    hstate = A("hstate", [P, IC * DS], F32)
    tails = A("tails", [P, IC * 4], BF16)
    xproj_sb = A("xproj_sb", [P, IC * 96], BF16)
    dtw_sb = A("dtw_sb", [DTR, DI], BF16)
    ow_stg = A("ow_stg", [P, 2 * FC * P], BF16)
    rows = A("rows", [1, 8 * 512], F32)
    m1p = A("m1p", [P, 8], F32)
    m2p = A("m2p", [P, 8], F32)
    dpk = A("dpk", [P, 8], F32)
    g1p = A("g1p", [P, 8], F32)
    g2p = A("g2p", [P, 8], F32)
    ltm8 = A("ltm8", [P, 64], F32)
    l2t8 = A("l2t8", [P, 64], F32)
    mk18 = A("mk18", [P, 64], F32)
    mk28 = A("mk28", [P, 64], F32)
    gtm = A("gtm", [P, 8], F32)
    logit_sb = A("logit_sb", [NE, FCH], F32)
    iw_stg = A("iw_stg", [P, 2 * FC * 512], BF16)
    wi_stg = A("wi_stg", [P, 2 * FC * 512], BF16)
    wo_stg = A("wo_stg", [P, 2 * 4 * 1024], BF16)
    mo_stg = A("mo_stg", [P, 2 * 512], BF16)
    gbc = A("gbc", [P, 2 * FCH], BF16)
    x1t = A("x1t", [P, 2 * CH], F32)
    sstg = A("sstg", [P, 2 * CH], F32)
    ystg = A("ystg", [P, 2 * 512], F32)
    xdblb = A("xdblb", [P, CH], BF16)

    R0 = mk._soff
    x_bf = A("x_bf", [P, FC * FCH], BF16, at=R0)
    mk._soff = R0 + FC * FCH * 2
    x_ln = A("x_ln", [P, FC * FCH], BF16)
    y_acc = A("y_acc", [P, FC * FCH], BF16)
    h_blk = A("h_blk", [P, FC * FCH], BF16)
    ygt = A("ygt", [P, 2 * 512], F32)
    ffn_end = mk._soff

    mk._soff = R0
    x1bf = A("x1bf", [P, FC * CH], BF16, at=R0)
    mk._soff = R0 + FC * CH * 2
    h_sb = A("h_sb", [P, FC * CH], BF16)
    z_off = mk._soff
    z_sb = A("z_sb", [P, IC * CH], BF16)
    xc = A("xc", [P, IC * CH], BF16)
    xib = A("xib", [P, 2 * 520], BF16)
    mo = A("mo", [P, FC * CH], BF16, at=z_off)
    delt = A("delt", [P, TS], F32)
    duv = A("duv", [P, TS], F32)
    a_s = A("a_s", [P, 2 * TS], F32)
    b_s = A("b_s", [P, 2 * TS], F32)
    h_out = A("h_out", [P, DS * TS], F32)
    Bbc = A("Bbc", [P, DS * TS], BF16)
    Cbc = A("Cbc", [P, DS * TS], BF16)
    b_all = A("b_all", [P, 8 * TS], F32)
    yscan = A("yscan", [P, CH], F32)
    ytmp = A("ytmp", [P, CH], F32)
    mam_end = mk._soff
    yg = A("yg", [P, IC * CH], BF16, at=R0)
    assert max(ffn_end, mam_end) <= int(nc.sbuf_top) - 64, (ffn_end, mam_end, nc.sbuf_top)

    ps = []
    for i in range(8):
        cm = nc.psum_tensor("psb%d" % i, [P, 512], F32)
        ps.append(cm.__enter__().ap())
        mk._ctxs.append(cm)

    s_pc = mk.sem("s_pc")
    s_cl = mk.sem("s_cl")
    s_xg = mk.sem("s_xg")
    s_hl = mk.sem("s_hl")
    s_ac = mk.sem("s_ac")
    s_st = mk.sem("s_st")
    s_bc = mk.sem("s_bc")
    s_pe = mk.sem("s_pe")
    s_dv = mk.sem("s_dv")
    s_sc = mk.sem("s_sc")
    s_po = mk.sem("s_po")

    def dve(build, waits=(), edge=False):
        w = list(waits)
        if edge:
            w.append((s_dv, s_dv.n))
        mk.op("vector", build, w, s_dv, 1)

    def act(build, waits=(), edge=False):
        w = list(waits)
        if edge:
            w.append((s_sc, s_sc.n))
        mk.op("scalar", build, w, s_sc, 1)

    def pool(build, waits=(), edge=False):
        w = list(waits)
        if edge:
            w.append((s_po, s_po.n))
        mk.op("gpsimd", build, w, s_po, 1)

    def mmgroup(lhsT_aps, rhs_aps, out_ap, waits=(), inc=None):
        n = len(lhsT_aps)
        for i, (lt, rh) in enumerate(zip(lhsT_aps, rhs_aps)):
            w = waits if i == 0 else ()
            ic = inc if i == n - 1 else None

            def b(e, lt=lt, rh=rh, st=(i == 0), sp=(i == n - 1), o=out_ap):
                return e.matmul(o, lt, rh, start=st, stop=sp)
            mk.op("tensor", b, w, ic, 1)

    rotA = Rot([ps[0], ps[1]])
    rotB = Rot([ps[2], ps[3]])

    st_iw = Stream(mk, "sync",
                   [iw_stg[:, ts_(j, FC * 512)].rearrange("p (k m) -> p k m", k=FC)
                    for j in range(2)], "iw")
    st_wi = Stream(mk, "sync",
                   [wi_stg[:, ts_(j, FC * 512)].rearrange("p (k m) -> p k m", k=FC)
                    for j in range(2)], "wi")
    st_wo = Stream(mk, "sync",
                   [wo_stg[:, ts_(j, 4 * 1024)].rearrange("p (k m) -> p k m", k=4)
                    for j in range(2)], "wo")
    st_mo = Stream(mk, "sync", [mo_stg[:, ts_(j, 512)] for j in range(2)], "mo")
    st_gb = Stream(mk, "sync", [gbc[:, ts_(j, FCH)] for j in range(2)], "gb")
    st_x1 = Stream(mk, "sync", [x1t[:, ts_(j, CH)] for j in range(2)], "x1")
    st_ow = Stream(mk, "sync",
                   [ow_stg[:, ts_(j, FC * P)].rearrange("p (k m) -> p k m", k=FC)
                    for j in range(2)], "ow")

    ss_prev = [None, None]
    ys_prev = [None, None]
    ab_free = [0, 0]

    for dst, src in ((identf, ident_in), (constf, const_in), (ones1, ones1_in),
                     (gam, gam_in), (bet, bet_in), (outb, outb_in)):
        mk.dma("sync", dst, src, inc=s_cl)
    mk.dma("gpsimd", oneskb[:, 0:1], onesk_in, inc=s_pc)
    mk.dma("gpsimd", wg_sb.rearrange("p (k e) -> p k e", k=FC),
           wgT.rearrange("(k p) e -> p k e", p=P), inc=s_pc)
    for e_i in range(NE):
        mk.dma("gpsimd", wiB[e_i], wiT[e_i], inc=s_pc)
        mk.dma("gpsimd", woB[e_i], woT[e_i], inc=s_pc)
    for d_ in range(2):
        mk.dma("gpsimd", inwB[d_], inwT[d_], inc=s_pc)
        mk.dma("gpsimd", xprojB[d_], xprojT[d_], inc=s_pc)
        mk.dma("gpsimd", dtwB[d_], dtwT[d_], inc=s_pc)
        mk.dma("gpsimd", moutB[d_], moutT[d_], inc=s_pc)
    mk.dma("gpsimd", outwB, outwT, inc=s_pc)
    PC0 = s_pc.n
    CL0 = s_cl.n

    def row(off, n):
        return rows[0:1, off:off + n]

    def stats_mms(xv, tcount, sqv, xwait):
        act(lambda e: e.activation(out=sqv, in_=xv, func=AF.Square),
            waits=[xwait], edge=True)
        sqsc = (s_sc, s_sc.n)
        nh = tcount // 512
        banks = [ps[4], ps[5], ps[6], ps[7]]
        bi = 0
        for q, (src, swt, ro) in enumerate(((xv, xwait, 0), (sqv, sqsc, 1024))):
            for h in range(nh):
                bank = banks[bi]
                bi += 1
                w = [swt, (s_sc, s_sc.n)]
                mmgroup([oneskb[:, 0:1]] * FC,
                        [src[:, c, ts_(h, 512)] for c in range(FC)],
                        bank[0:1, :], waits=w, inc=s_pe)
                pv = (s_pe, s_pe.n)
                act(lambda e, bank=bank, ro=ro, h=h:
                    e.activation(out=row(ro + h * 512, 512), in_=bank[0:1, :],
                                 func=AF.Copy), waits=[pv])
        return (s_sc, s_sc.n)

    def ln_rows(tcount, attn, swait):
        m = row(0, tcount)
        v = row(1024, tcount)
        ta = row(2048, tcount)
        tb = row(3072, tcount)
        c = 1.0 / D
        dve(lambda e: e.tensor_scalar_mul(out=m, in0=m, scalar1=c),
            waits=[swait], edge=True)
        dve(lambda e: e.tensor_tensor(out=ta, in0=m, in1=m, op=ALU.mult),
            edge=True)
        dve(lambda e: e.scalar_tensor_tensor(out=v, in0=v, scalar=c, in1=ta,
                                             op0=ALU.mult, op1=ALU.subtract),
            edge=True)
        vdv = (s_dv, s_dv.n)
        act(lambda e: e.activation(out=ta, in_=v, func=AF.Ln, bias=EPS),
            waits=[vdv], edge=True)
        act(lambda e: e.activation(out=tb, in_=ta, func=AF.Exp, scale=-0.5),
            edge=True)
        r1sc = (s_sc, s_sc.n)
        if not attn:
            Arow, Brow = tb, ta
            dve(lambda e, Arow=Arow, Brow=Brow: e.scalar_tensor_tensor(
                out=Brow, in0=m, scalar=-1.0, in1=Arow, op0=ALU.mult,
                op1=ALU.mult), waits=[r1sc], edge=True)
        else:
            tv = row(512, tcount)
            dve(lambda e: e.tensor_scalar_add(out=tv, in0=v, scalar1=EPS),
                waits=[r1sc], edge=True)
            dve(lambda e: e.reciprocal(out=ta, in_=tv), edge=True)
            dve(lambda e: e.tensor_tensor(out=tv, in0=v, in1=ta, op=ALU.mult),
                edge=True)
            v2dv = (s_dv, s_dv.n)
            act(lambda e: e.activation(out=ta, in_=tv, func=AF.Ln, bias=EPS),
                waits=[v2dv], edge=True)
            act(lambda e: e.activation(out=tv, in_=ta, func=AF.Exp, scale=-0.5),
                edge=True)
            r2sc = (s_sc, s_sc.n)
            Arow = row(2048, tcount)
            Brow = row(3584, tcount)
            dve(lambda e, Arow=Arow: e.tensor_tensor(
                out=Arow, in0=tb, in1=tv, op=ALU.mult), waits=[r2sc], edge=True)
            dve(lambda e, Arow=Arow, Brow=Brow: e.scalar_tensor_tensor(
                out=Brow, in0=m, scalar=-1.0, in1=Arow, op0=ALU.mult,
                op1=ALU.mult), edge=True)
        return Arow, Brow, (s_dv, s_dv.n), (s_sc, s_sc.n)

    def bcast_rows(Arow, Brow, tcount, wdv, wsc):
        out = []
        nh = tcount // 512
        for h in range(nh):
            bA = ps[4 + 2 * h]
            bB = ps[5 + 2 * h]
            mmgroup([ones1], [Arow[0:1, ts_(h, 512)]], bA[:, 0:512],
                    waits=[wdv, wsc, (s_sc, s_sc.n)], inc=s_pe)
            mmgroup([ones1], [Brow[0:1, ts_(h, 512)]], bB[:, 0:512], inc=s_pe)
            out.append((bA, bB))
        return out, (s_pe, s_pe.n)

    def ffn_phase(p2, seq, sc):
        t0 = sc * FCH
        trg = slice(t0, t0 + FCH)
        src_d = out_d if p2 else x_in
        acc_d = out_d if p2 else x1S
        xv = x_bf.rearrange("p (c t) -> p c t", c=FC)
        lv = x_ln.rearrange("p (c t) -> p c t", c=FC)
        hv = h_blk.rearrange("p (c t) -> p c t", c=FC)
        yv = y_acc.rearrange("p (c t) -> p c t", c=FC)

        mk.fence()
        if not p2:
            mk.dma("sync", x1S[seq][:, trg], x_in[seq][:, trg], inc=s_st)
        mk.dma("gpsimd", xv,
               src_d[seq].rearrange("(c p) t -> p c t", p=P)[:, :, trg],
               waits=[(s_pc, PC0)], inc=s_xg)
        xw = (s_xg, s_xg.n)

        sw = stats_mms(xv, FCH, hv, xw)
        Arow, Brow, wdv, wsc = ln_rows(FCH, False, sw)
        bc, wpe = bcast_rows(Arow, Brow, FCH, wdv, wsc)
        for h in range(2):
            bA, bB = bc[h]
            for c in range(FC):
                dve(lambda e, c=c, h=h, bA=bA: e.tensor_tensor(
                    out=ygt[:, 0:512], in0=xv[:, c, ts_(h, 512)], in1=bA[:, 0:512],
                    op=ALU.mult),
                    waits=[wpe, xw] if (c == 0 and h == 0) else (), edge=True)
                dve(lambda e, c=c, h=h, bB=bB: e.tensor_tensor(
                    out=lv[:, c, ts_(h, 512)], in0=ygt[:, 0:512], in1=bB[:, 0:512],
                    op=ALU.add), edge=True)
        xln_dv = (s_dv, s_dv.n)

        for h in range(2):
            mmgroup([wg_sb[:, ts_(c, NE)] for c in range(FC)],
                    [lv[:, c, ts_(h, 512)] for c in range(FC)],
                    ps[6][0:NE, :], waits=[xln_dv, (s_sc, s_sc.n)], inc=s_pe)
            pv = (s_pe, s_pe.n)
            act(lambda e, h=h: e.activation(out=logit_sb[:, ts_(h, 512)],
                                            in_=ps[6][0:NE, :], func=AF.Copy),
                waits=[pv])
        lg_sc = (s_sc, s_sc.n)
        for tt in range(8):
            mk.op("tensor", lambda e, tt=tt: e.transpose(
                ps[7][:, 0:NE], logit_sb[0:NE, ts_(tt, P)], identf[0:NE, 0:NE]),
                waits=[lg_sc] if tt == 0 else [(s_sc, s_sc.n)], inc=s_pe)
            pv = (s_pe, s_pe.n)
            act(lambda e, tt=tt: e.activation(out=ltm8[:, ts_(tt, NE)],
                                              in_=ps[7][:, 0:NE], func=AF.Copy),
                waits=[pv])
        ev_sc = (s_sc, s_sc.n)
        for tt in range(8):
            dve(lambda e, tt=tt: e.tensor_reduce(
                out=m1p[:, tt:tt + 1], in_=ltm8[:, ts_(tt, NE)], axis=AX.X,
                op=ALU.max), waits=[ev_sc] if tt == 0 else ())
        dve(lambda e: e.tensor_copy(out=gtm, in_=m1p), edge=True)
        for tt in range(8):
            dve(lambda e, tt=tt: e.tensor_scalar(
                out=mk18[:, ts_(tt, NE)], in0=ltm8[:, ts_(tt, NE)],
                scalar1=m1p[:, tt:tt + 1], scalar2=None, op0=ALU.is_equal))
            dve(lambda e, tt=tt: e.scalar_tensor_tensor(
                out=l2t8[:, ts_(tt, NE)], in0=mk18[:, ts_(tt, NE)], scalar=-1e30,
                in1=ltm8[:, ts_(tt, NE)], op0=ALU.mult, op1=ALU.add), edge=True)
        for tt in range(8):
            dve(lambda e, tt=tt: e.tensor_reduce(
                out=m2p[:, tt:tt + 1], in_=l2t8[:, ts_(tt, NE)], axis=AX.X,
                op=ALU.max), edge=(tt == 0))
        dve(lambda e: e.tensor_tensor(out=dpk, in0=m1p, in1=m2p,
                                      op=ALU.subtract), edge=True)
        ddv = (s_dv, s_dv.n)
        act(lambda e: e.activation(out=g1p, in_=dpk, func=AF.Sigmoid),
            waits=[ddv], edge=True)
        sg = (s_sc, s_sc.n)
        dve(lambda e: e.tensor_scalar(out=g2p, in0=g1p, scalar1=-1.0,
                                      scalar2=1.0, op0=ALU.mult, op1=ALU.add),
            waits=[sg], edge=True)
        dve(lambda e: e.tensor_copy(out=gtm, in_=g2p), edge=True)
        for tt in range(8):
            dve(lambda e, tt=tt: e.tensor_scalar(
                out=mk28[:, ts_(tt, NE)], in0=l2t8[:, ts_(tt, NE)],
                scalar1=m2p[:, tt:tt + 1], scalar2=None, op0=ALU.is_equal))
            dve(lambda e, tt=tt: e.tensor_scalar_mul(
                out=mk18[:, ts_(tt, NE)], in0=mk18[:, ts_(tt, NE)],
                scalar1=g1p[:, tt:tt + 1]))
            dve(lambda e, tt=tt: e.tensor_scalar_mul(
                out=mk28[:, ts_(tt, NE)], in0=mk28[:, ts_(tt, NE)],
                scalar1=g2p[:, tt:tt + 1]), edge=True)
            dve(lambda e, tt=tt: e.tensor_tensor(
                out=gtm, in0=mk18[:, ts_(tt, NE)], in1=mk28[:, ts_(tt, NE)],
                op=ALU.add), edge=True)
            gdv = (s_dv, s_dv.n)
            mk.op("tensor", lambda e, tt=tt: e.transpose(
                ps[7][0:NE, 0:P], gtm, identf),
                waits=[gdv, (s_sc, s_sc.n)], inc=s_pe)
            pv = (s_pe, s_pe.n)
            act(lambda e, tt=tt: e.activation(
                out=logit_sb[0:NE, ts_(tt, P)], in_=ps[7][0:NE, 0:P],
                func=AF.Copy), waits=[pv])
        g_sc = (s_sc, s_sc.n)
        mk.dma("gpsimd", gS, logit_sb, waits=[g_sc], inc=s_xg)
        g_xg = (s_xg, s_xg.n)

        for e_i in range(NE):
            gslot, gpv = st_gb.put(gS[e_i:e_i + 1, :].to_broadcast([P, FCH]),
                                   extra=[g_xg])
            for hb in range(4):
                wi_aps = []
                for sh in range(2):
                    mc = slice(hb * 1024 + sh * 512, hb * 1024 + (sh + 1) * 512)
                    wi_aps.append(st_wi.put(
                        wiB[e_i].rearrange("(k p) m -> p k m", p=P)[:, :, mc],
                        extra=[(s_pc, PC0)]))
                wo_aps = []
                for sh in range(2):
                    k0 = (hb * 1024 + sh * 512) // P
                    wo_aps.append(st_wo.put(
                        woB[e_i].rearrange("(k p) m -> p k m", p=P)[:, k0:k0 + 4, :],
                        extra=[(s_pc, PC0)]))
                for mt in range(8):
                    sap, spv = wi_aps[mt // 4]
                    lhs = sap[:, :, ts_(mt % 4, P)]
                    for nh in range(2):
                        bank, bw = rotA.get()
                        w = [spv] + bw
                        if mt == 0 and nh == 0:
                            w.append(xln_dv)
                        mmgroup([lhs[:, c, :] for c in range(FC)],
                                [lv[:, c, ts_(nh, 512)] for c in range(FC)],
                                bank, waits=w, inc=s_pe)
                        pv = (s_pe, s_pe.n)
                        act(lambda e, bank=bank, mt=mt, nh=nh: e.activation(
                            out=hv[:, mt, ts_(nh, 512)], in_=bank, func=AF.Copy),
                            waits=[pv])
                        rotA.free((s_sc, s_sc.n))
                st_wi.done((s_pe, s_pe.n))
                st_wi.done((s_pe, s_pe.n))
                h_sc = (s_sc, s_sc.n)
                for m in range(FC):
                    for nh in range(2):
                        bank, bw = rotB.get()
                        lhs_list = []
                        rhs_list = []
                        for sh in range(2):
                            sap2, spv2 = wo_aps[sh]
                            for kc in range(4):
                                lhs_list.append(sap2[:, kc, ts_(m, P)])
                                rhs_list.append(hv[:, sh * 4 + kc, ts_(nh, 512)])
                        w = bw + ([wo_aps[1][1], h_sc]
                                  if (m == 0 and nh == 0) else [])
                        mmgroup(lhs_list, rhs_list, bank, waits=w, inc=s_pe)
                        pv = (s_pe, s_pe.n)
                        dve(lambda e, bank=bank, nh=nh, gslot=gslot:
                            e.tensor_tensor(out=ygt[:, ts_(nh, 512)], in0=bank,
                                            in1=gslot[:, ts_(nh, 512)],
                                            op=ALU.mult), waits=[pv, gpv])
                        rotB.free((s_dv, s_dv.n))
                        if e_i == 0 and hb == 0:
                            def bb(e, m=m, nh=nh):
                                return e.tensor_scalar_mul(
                                    out=yv[:, m, ts_(nh, 512)],
                                    in0=ygt[:, ts_(nh, 512)], scalar1=0.5)
                        else:
                            def bb(e, m=m, nh=nh):
                                return e.scalar_tensor_tensor(
                                    out=yv[:, m, ts_(nh, 512)],
                                    in0=ygt[:, ts_(nh, 512)], scalar=0.5,
                                    in1=yv[:, m, ts_(nh, 512)],
                                    op0=ALU.mult, op1=ALU.add)
                        dve(bb, edge=True)
                st_wo.done((s_pe, s_pe.n))
                st_wo.done((s_pe, s_pe.n))
            st_gb.done((s_dv, s_dv.n))
        yfin = (s_dv, s_dv.n)
        for m in range(FC):
            for nh in range(2):
                j = (m * 2 + nh) % 2
                w = [yfin, ys_prev[j]]
                dve(lambda e, m=m, nh=nh, j=j: e.tensor_copy(
                    out=ystg[:, ts_(j, 512)], in_=yv[:, m, ts_(nh, 512)]),
                    waits=w)
                cdv = (s_dv, s_dv.n)
                mk.dma("gpsimd",
                       acc_d[seq][ts_(m, P), t0 + nh * 512:t0 + (nh + 1) * 512],
                       ystg[:, ts_(j, 512)], waits=[cdv, (s_st, s_st.n)],
                       inc=s_ac, accum=True)
                ys_prev[j] = (s_ac, s_ac.n)

    def mamba_phase(p2, seq, k, first_of_seq):
        dir_ = 1 if p2 else 0
        rev = p2
        t0 = k * CH
        trg = slice(t0, t0 + CH)
        hvv = h_sb.rearrange("p (c t) -> p c t", c=FC)
        xcv = xc.rearrange("p (c t) -> p c t", c=IC)
        zvv = z_sb.rearrange("p (c t) -> p c t", c=IC)
        ygv = yg.rearrange("p (c t) -> p c t", c=IC)
        mov = mo.rearrange("p (c t) -> p c t", c=FC)
        hov = h_out.rearrange("p (s t) -> p s t", s=DS)
        Bv = Bbc.rearrange("p (s t) -> p s t", s=DS)
        Cv = Cbc.rearrange("p (s t) -> p s t", s=DS)

        mk.fence()
        if first_of_seq:
            dve(lambda e: e.memset(hstate, 0.0))
            dve(lambda e: e.memset(tails.bitcast(F32), 0.0))
            dve(lambda e: e.tensor_copy(out=gtm, in_=m1p), edge=True)
        init_dv = (s_dv, s_dv.n)

        if not p2:
            x1v = x1bf.rearrange("p (c t) -> p c t", c=FC)
            mk.dma("gpsimd", x1v,
                   x1S[seq].rearrange("(c p) t -> p c t", p=P)[:, :, trg],
                   waits=[(s_ac, s_ac.n)], inc=s_xg)
            xw = (s_xg, s_xg.n)
            sw = stats_mms(x1v, CH, zvv[:, 0:FC, :], xw)
            Arow, Brow, wdv, wsc = ln_rows(CH, True, sw)
            bc, wpe = bcast_rows(Arow, Brow, CH, wdv, wsc)
            bA, bB = bc[0]
            for c in range(FC):
                dve(lambda e, c=c, bA=bA: e.tensor_tensor(
                    out=yscan, in0=x1v[:, c, :], in1=bA[:, 0:512], op=ALU.mult),
                    waits=[wpe, xw] if c == 0 else (), edge=True)
                dve(lambda e, bB=bB: e.tensor_tensor(
                    out=yscan, in0=yscan, in1=bB[:, 0:512], op=ALU.add),
                    edge=True)
                dve(lambda e, c=c: e.tensor_scalar(
                    out=hvv[:, c, :], in0=yscan, scalar1=gam[:, c:c + 1],
                    scalar2=bet[:, c:c + 1], op0=ALU.mult, op1=ALU.add),
                    edge=True)
            h_rdy = (s_dv, s_dv.n)
            dve(lambda e: e.scalar_tensor_tensor(
                out=rows[0:1, 1536:2048], in0=rows[0:1, 0:512], scalar=-1.0,
                in1=rows[0:1, 3072:3584], op0=ALU.mult, op1=ALU.mult),
                edge=True)
            b1_dv = (s_dv, s_dv.n)
            mk.dma("sync", hS[seq].rearrange("(c p) t -> p c t", p=P)[:, :, trg],
                   hvv, waits=[h_rdy], inc=s_st)
            h_saved = (s_st, s_st.n)
        else:
            mk.dma("sync", hvv,
                   hS[seq].rearrange("(c p) t -> p c t", p=P)[:, :, trg],
                   inc=s_hl)
            h_rdy = (s_hl, s_hl.n)
            h_saved = None

        xi_prev = [None, None]
        last_silu = None
        for si in range(8):
            mc = ts_(si, 512)
            sap, spv = st_iw.put(
                inwB[dir_].rearrange("(k p) m -> p k m", p=P)[:, :, mc],
                extra=[(s_pc, PC0)])
            for mt in range(4):
                gi = si * 4 + mt
                lhs = sap[:, :, ts_(mt, P)]
                bank, bw = rotA.get()
                mmgroup([lhs[:, c, :] for c in range(FC)],
                        [hvv[:, c, :] for c in range(FC)],
                        bank, waits=[spv, h_rdy] + bw, inc=s_pe)
                pv = (s_pe, s_pe.n)
                if gi < IC:
                    dc = gi
                    j = dc % 2
                    dst = (xib[:, j * 520:j * 520 + 512] if rev
                           else xib[:, j * 520 + 3:j * 520 + 515])
                    dve(lambda e, bank=bank, dst=dst: e.tensor_copy(
                        out=dst, in_=bank), waits=[pv, xi_prev[j]])
                    rotA.free((s_dv, s_dv.n))
                    xidv = (s_dv, s_dv.n)
                    if rev:
                        pool(lambda e, j=j, dc=dc: e.tensor_copy(
                            out=xib[:, j * 520 + 512:j * 520 + 515],
                            in_=tails[:, dc * 4:dc * 4 + 3]),
                            waits=[xidv, init_dv], edge=True)
                        offs = [3, 2, 1, 0]
                    else:
                        pool(lambda e, j=j, dc=dc: e.tensor_copy(
                            out=xib[:, j * 520:j * 520 + 3],
                            in_=tails[:, dc * 4:dc * 4 + 3]),
                            waits=[xidv, init_dv], edge=True)
                        offs = [0, 1, 2, 3]
                    for ki in range(4):
                        o = offs[ki]
                        wv = cwv[:, dc * 4 + ki:dc * 4 + ki + 1]
                        if ki == 0:
                            pool(lambda e, j=j, o=o, wv=wv: e.tensor_tensor(
                                out=ytmp, in0=xib[:, j * 520 + o:j * 520 + o + 512],
                                in1=wv.to_broadcast([P, CH]), op=ALU.mult),
                                waits=[last_silu], edge=True)
                        else:
                            pool(lambda e, j=j, o=o, wv=wv: e.tensor_tensor(
                                out=yscan, in0=xib[:, j * 520 + o:j * 520 + o + 512],
                                in1=wv.to_broadcast([P, CH]), op=ALU.mult),
                                edge=True)
                            pool(lambda e: e.tensor_tensor(
                                out=ytmp, in0=ytmp, in1=yscan, op=ALU.add),
                                edge=True)
                    if rev:
                        pool(lambda e, j=j, dc=dc: e.tensor_copy(
                            out=tails[:, dc * 4:dc * 4 + 3],
                            in_=xib[:, j * 520:j * 520 + 3]), edge=True)
                    else:
                        pool(lambda e, j=j, dc=dc: e.tensor_copy(
                            out=tails[:, dc * 4:dc * 4 + 3],
                            in_=xib[:, j * 520 + 512:j * 520 + 515]), edge=True)
                    xi_prev[j] = (s_po, s_po.n)
                    podv = (s_po, s_po.n)
                    act(lambda e, dc=dc: e.activation(
                        out=xcv[:, dc, :], in_=ytmp, func=AF.Silu,
                        bias=cbv[:, dc:dc + 1]), waits=[podv])
                    last_silu = (s_sc, s_sc.n)
                else:
                    zc = gi - IC
                    act(lambda e, bank=bank, zc=zc: e.activation(
                        out=zvv[:, zc, :], in_=bank, func=AF.Silu), waits=[pv])
                    rotA.free((s_sc, s_sc.n))
            st_iw.done((s_pe, s_pe.n))
        xc_sc = (s_sc, s_sc.n)

        mmgroup([xproj_sb[:, ts_(c, 96)] for c in range(IC)],
                [xcv[:, c, :] for c in range(IC)],
                ps[5][0:96, :], waits=[xc_sc, (s_cl, CL0), (s_dv, s_dv.n)],
                inc=s_pe)
        pv = (s_pe, s_pe.n)
        dve(lambda e: e.tensor_copy(out=xdblb[0:96, :], in_=ps[5][0:96, :]),
            waits=[pv])
        xd_dv = (s_dv, s_dv.n)
        mk.dma("sync", bcS, xdblb[64:96, :],
               waits=[xd_dv, (s_bc, s_bc.n)], inc=s_st)
        bw0 = (s_st, s_st.n)

        subs = list(range(NSUB)) if not rev else list(range(NSUB - 1, -1, -1))
        ball = b_all.rearrange("p (s t) -> p s t", s=8)
        prodq_prev = [None] * 4
        sdvq_prev = [None] * 4
        bq2_prev = None
        yg_prev = None
        for isub, sub in enumerate(subs):
            tsl = ts_(sub, TS)
            bcw = [bw0]
            if isub > 0:
                bcw.append((s_dv, s_dv.n))
                bcw.append((s_po, s_po.n))
            mk.dma("sync", Bv,
                   bcS[0:16, tsl].unsqueeze(0).to_broadcast([P, DS, TS]),
                   waits=bcw, inc=s_bc)
            mk.dma("sync", Cv,
                   bcS[16:32, tsl].unsqueeze(0).to_broadcast([P, DS, TS]),
                   waits=[bw0] + ([(s_po, s_po.n)] if isub else []), inc=s_bc)
            bc_rdy = (s_bc, s_bc.n)
            dt_pv = []
            for dg in range(IC):
                mmgroup([dtw_sb[:, ts_(dg, P)]], [xdblb[0:DTR, tsl]],
                        ps[dg // 2][:, ts_(dg % 2, TS)],
                        waits=([xd_dv, (s_dv, s_dv.n), (s_sc, s_sc.n)]
                               if dg == 0 else []), inc=s_pe)
                dt_pv.append((s_pe, s_pe.n))
            for dg in range(IC):
                dsl = ps[dg // 2][:, ts_(dg % 2, TS)]
                dve(lambda e, dsl=dsl, dg=dg: e.tensor_scalar(
                    out=ytmp[:, 0:TS], in0=dsl,
                    scalar1=dtb[:, dg:dg + 1], scalar2=None, op0=ALU.add),
                    waits=[dt_pv[dg]], edge=True)
                dve(lambda e, dsl=dsl, dg=dg: e.tensor_scalar(
                    out=delt, in0=dsl, scalar1=dtb[:, dg:dg + 1],
                    scalar2=0.0, op0=ALU.add, op1=ALU.max))
                ac2 = (s_dv, s_dv.n)
                dve(lambda e: e.scalar_tensor_tensor(
                    out=ytmp[:, 0:TS], in0=delt, scalar=-2.0, in1=ytmp[:, 0:TS],
                    op0=ALU.mult, op1=ALU.add), waits=[ac2], edge=True)
                nudv = (s_dv, s_dv.n)
                act(lambda e: e.activation(out=ytmp[:, TS:2 * TS],
                                           in_=ytmp[:, 0:TS], func=AF.Exp),
                    waits=[nudv], edge=True)
                act(lambda e: e.activation(out=ytmp[:, 0:TS],
                                           in_=ytmp[:, TS:2 * TS], func=AF.Ln,
                                           bias=1.0), edge=True)
                spsc = (s_sc, s_sc.n)
                dve(lambda e: e.tensor_tensor(out=delt, in0=delt,
                                              in1=ytmp[:, 0:TS], op=ALU.add),
                    waits=[spsc], edge=True)
                dve(lambda e, dg=dg, tsl=tsl: e.tensor_tensor(
                    out=duv, in0=delt, in1=xcv[:, dg, tsl], op=ALU.mult),
                    waits=[bq2_prev], edge=True)
                du_dv = (s_dv, s_dv.n)
                del_dv = du_dv
                bq = [None] * 4
                for qb in range(2):
                    pool(lambda e, qb=qb: e.tensor_tensor(
                        out=ball[:, 4 * qb:4 * qb + 4, :],
                        in0=duv.unsqueeze(1).to_broadcast([P, 4, TS]),
                        in1=Bv[:, 4 * qb:4 * qb + 4, :], op=ALU.mult),
                        waits=[du_dv, sdvq_prev[qb + 2]]
                        + ([bc_rdy] if qb == 0 and dg == 0 else []))
                    bq[qb] = (s_po, s_po.n)
                saveq = []
                idx = 0 if rev else TS - 1
                for s_i in range(DS):
                    jj = s_i % 2
                    lane = (s_i // 4 % 2) * 4 + s_i % 4
                    asl = a_s[:, ts_(jj, TS)]
                    act(lambda e, dg=dg, s_i=s_i, asl=asl: e.activation(
                        out=asl, in_=delt, func=AF.Exp,
                        scale=Aneg[:, dg * DS + s_i:dg * DS + s_i + 1]),
                        waits=[del_dv, (s_dv, ab_free[jj])])
                    a_sc = (s_sc, s_sc.n)
                    if rev:
                        def sb(e, s_i=s_i, asl=asl, dg=dg, lane=lane):
                            return e.tensor_tensor_scan(
                                out=hov[:, s_i, ::-1], data0=asl[:, ::-1],
                                data1=ball[:, lane, ::-1],
                                initial=hstate[:, dg * DS + s_i:dg * DS + s_i + 1],
                                op0=ALU.mult, op1=ALU.add)
                    else:
                        def sb(e, s_i=s_i, asl=asl, dg=dg, lane=lane):
                            return e.tensor_tensor_scan(
                                out=hov[:, s_i, :], data0=asl,
                                data1=ball[:, lane, :],
                                initial=hstate[:, dg * DS + s_i:dg * DS + s_i + 1],
                                op0=ALU.mult, op1=ALU.add)
                    dve(sb, waits=[a_sc, bq[s_i // 4], prodq_prev[s_i // 4]],
                        edge=True)
                    ab_free[jj] = s_dv.n
                    sdvq_prev[s_i // 4] = (s_dv, s_dv.n)
                    if s_i % 4 == 3:
                        q = s_i // 4
                        dve(lambda e, dg=dg, q=q, idx=idx: e.tensor_copy(
                            out=hstate[:, dg * DS + 4 * q:dg * DS + 4 * q + 4],
                            in_=hov[:, 4 * q:4 * q + 4, idx]), edge=True)
                        saveq.append((s_dv, s_dv.n))
                        pool(lambda e, q=q: e.tensor_tensor(
                            out=h_out[:, 4 * q * TS:(4 * q + 4) * TS],
                            in0=h_out[:, 4 * q * TS:(4 * q + 4) * TS],
                            in1=Cbc[:, 4 * q * TS:(4 * q + 4) * TS],
                            op=ALU.mult),
                            waits=[saveq[-1]]
                            + ([bc_rdy] if q == 0 and dg == 0 else []))
                        prodq_prev[q] = (s_po, s_po.n)
                        if q == 0:
                            pool(lambda e: e.tensor_tensor(
                                out=ball[:, 0:4, :],
                                in0=duv.unsqueeze(1).to_broadcast([P, 4, TS]),
                                in1=Bv[:, 8:12, :], op=ALU.mult),
                                waits=[sdvq_prev[0]])
                            bq[2] = (s_po, s_po.n)
                        elif q == 1:
                            dve(lambda e: e.tensor_tensor(
                                out=ball[:, 4:8, :],
                                in0=duv.unsqueeze(1).to_broadcast([P, 4, TS]),
                                in1=Bv[:, 12:16, :], op=ALU.mult),
                                waits=[bc_rdy] if dg == 0 else (), edge=True)
                            bq[3] = (s_dv, s_dv.n)
                bq2_prev = bq[2]
                parts = [yscan[:, tsl], b_s[:, 0:TS], b_s[:, TS:2 * TS], duv]
                for q in range(4):
                    dve(lambda e, q=q, pt=parts[q]: e.tensor_reduce(
                        out=pt.unsqueeze(2),
                        in_=hov[:, 4 * q:4 * q + 4, :].transpose([0, 2, 1]),
                        axis=AX.X, op=ALU.add), waits=[prodq_prev[q]])
                dve(lambda e, tsl=tsl: e.tensor_tensor(
                    out=yscan[:, tsl], in0=yscan[:, tsl], in1=b_s[:, 0:TS],
                    op=ALU.add), edge=True)
                dve(lambda e: e.tensor_tensor(
                    out=b_s[:, TS:2 * TS], in0=b_s[:, TS:2 * TS], in1=duv,
                    op=ALU.add), edge=True)
                dve(lambda e, tsl=tsl: e.tensor_tensor(
                    out=yscan[:, tsl], in0=yscan[:, tsl],
                    in1=b_s[:, TS:2 * TS], op=ALU.add), edge=True)
                dve(lambda e, dg=dg, tsl=tsl: e.scalar_tensor_tensor(
                    out=ytmp[:, 0:TS], in0=xcv[:, dg, tsl],
                    scalar=Dv[:, dg:dg + 1], in1=yscan[:, tsl], op0=ALU.mult,
                    op1=ALU.add), edge=True)
                ygw = [h_saved] if (dg == 0 and isub == 0 and not p2) else []
                dve(lambda e, dg=dg, tsl=tsl: e.tensor_tensor(
                    out=ygv[:, dg, tsl], in0=ytmp[:, 0:TS], in1=zvv[:, dg, tsl],
                    op=ALU.mult), waits=ygw, edge=True)
                yg_prev = (s_dv, s_dv.n)
        yg_dv = (s_dv, s_dv.n)

        banks = [ps[0], ps[1], ps[2], ps[3]]
        for mh in range(2):
            first_w = [yg_dv, (s_sc, s_sc.n), (s_dv, s_dv.n)]
            for kdg in range(IC):
                sap, spv = st_mo.put(
                    moutB[dir_].rearrange("(k p) m -> p k m", p=P)
                    [:, kdg, ts_(mh, 512)], extra=[(s_pc, PC0)])
                for m in range(4):
                    w = [spv] + (first_w if kdg == 0 else [])

                    def b(e, sap=sap, m=m, kdg=kdg):
                        return e.matmul(banks[m], sap[:, ts_(m, P)],
                                        ygv[:, kdg, :], start=(kdg == 0),
                                        stop=(kdg == IC - 1))
                    mk.op("tensor", b, w, s_pe if m == 3 else None, 1)
                st_mo.done((s_pe, s_pe.n))
            mm_pe = (s_pe, s_pe.n)
            for m in range(4):
                act(lambda e, m=m, mh=mh: e.activation(
                    out=mov[:, mh * 4 + m, :], in_=banks[m], func=AF.Copy),
                    waits=[mm_pe] if m == 0 else [])
        mo_sc = (s_sc, s_sc.n)

        src_res = x1S if not p2 else sS
        dst_res = sS if not p2 else out_d
        if not p2:
            mmgroup([ones1], [rows[0:1, 3072:3584]], ps[4][:, 0:512],
                    waits=[b1_dv, (s_dv, s_dv.n), (s_sc, s_sc.n)], inc=s_pe)
            mmgroup([ones1], [rows[0:1, 1536:2048]], ps[5][:, 0:512], inc=s_pe)
            lnb_pe = (s_pe, s_pe.n)
        for m in range(FC):
            xslot, xpv = st_x1.put(
                src_res[seq][ts_(m, P), trg],
                extra=[(s_ac, s_ac.n)] if not p2 else [])
            oslot, opv = st_ow.put(
                outwB[dir_ * D:(dir_ + 1) * D]
                .rearrange("(c p) m -> p c m", p=P)[:, :, ts_(m, P)],
                extra=[(s_pc, PC0)])
            bank, bw = rotB.get()
            w = ([mo_sc, (s_dv, s_dv.n)] if m == 0 else []) + bw + [opv]
            mmgroup([oslot[:, c, :] for c in range(FC)],
                    [mov[:, c, :] for c in range(FC)],
                    bank, waits=w, inc=s_pe)
            pv = (s_pe, s_pe.n)
            j = m % 2
            wts = [pv, xpv, ss_prev[j]]
            if not p2:
                dve(lambda e, bank=bank, j=j, m=m, xslot=xslot:
                    e.scalar_tensor_tensor(
                        out=sstg[:, ts_(j, CH)], in0=bank,
                        scalar=outb[:, m:m + 1], in1=xslot, op0=ALU.add,
                        op1=ALU.add), waits=wts + [lnb_pe])
                dve(lambda e, j=j, xslot=xslot: e.tensor_tensor(
                    out=ytmp, in0=xslot, in1=ps[4][:, 0:512], op=ALU.mult),
                    edge=True)
                dve(lambda e, j=j: e.tensor_tensor(
                    out=sstg[:, ts_(j, CH)], in0=sstg[:, ts_(j, CH)], in1=ytmp,
                    op=ALU.add), edge=True)
                dve(lambda e, j=j: e.tensor_tensor(
                    out=sstg[:, ts_(j, CH)], in0=sstg[:, ts_(j, CH)],
                    in1=ps[5][:, 0:512], op=ALU.add), edge=True)
            else:
                dve(lambda e, bank=bank, j=j, xslot=xslot: e.tensor_tensor(
                    out=sstg[:, ts_(j, CH)], in0=bank, in1=xslot, op=ALU.add),
                    waits=wts)
            rotB.free((s_dv, s_dv.n))
            st_x1.done((s_dv, s_dv.n))
            st_ow.done((s_pe, s_pe.n))
            sdv = (s_dv, s_dv.n)
            mk.dma("sync", dst_res[seq][ts_(m, P), trg], sstg[:, ts_(j, CH)],
                   waits=[sdv], inc=s_st)
            ss_prev[j] = (s_st, s_st.n)

    def pass_setup(dir_):
        for dst, src in ((Dv, d_in[dir_]), (dtb, dtb_in[dir_]),
                         (cbv, cb_in[dir_]), (cwv, cw_in[dir_])):
            mk.dma("sync", dst, src, waits=[(s_dv, s_dv.n), (s_po, s_po.n)],
                   inc=s_cl)
        mk.dma("sync", xproj_sb.rearrange("p (c m) -> p c m", c=IC),
               xprojB[dir_].rearrange("(c p) m -> p c m", p=P),
               waits=[(s_pc, PC0), (s_pe, s_pe.n)], inc=s_cl)
        mk.dma("sync", dtw_sb, dtwB[dir_], waits=[(s_pc, PC0)], inc=s_cl)
        mk.dma("sync", h_out[:, 0:IC * DS], alog[dir_],
               waits=[(s_dv, s_dv.n), (s_po, s_po.n)], inc=s_cl)
        clw = (s_cl, s_cl.n)
        act(lambda e: e.activation(out=Aneg, in_=h_out[:, 0:IC * DS],
                                   func=AF.Exp), waits=[clw], edge=True)
        asc = (s_sc, s_sc.n)
        dve(lambda e: e.tensor_scalar_mul(out=Aneg, in0=Aneg, scalar1=-1.0),
            waits=[asc], edge=True)
        mk.fence()

    pass_setup(0)
    for seq in range(NSEQ):
        for f in range(NSC):
            ffn_phase(False, seq, f)
            mamba_phase(False, seq, 2 * f, first_of_seq=(f == 0))
            mamba_phase(False, seq, 2 * f + 1, first_of_seq=False)
    pass_setup(1)
    for seq in range(NSEQ):
        for f in range(NSC - 1, -1, -1):
            mamba_phase(True, seq, 2 * f + 1, first_of_seq=(f == NSC - 1))
            mamba_phase(True, seq, 2 * f, first_of_seq=False)
            ffn_phase(True, seq, f)

    mk.fence(engines=("sync",))
    mk.op("sync", lambda e: e.nop())
    mk.emit()
    return mk.nc


_NC_CACHE = {}


def kernel(**inputs):
    xs = np.asarray(inputs["L0_attn_x"], np.float32)

    def f32(a):
        return np.ascontiguousarray(np.asarray(a, np.float32))

    w_gate = f32(inputs["L0_ffn_w_gate"])
    w_in = f32(inputs["L0_ffn_w_in"])
    w_out = f32(inputs["L0_ffn_w_out"])
    wgT = np.ascontiguousarray(w_gate.T)
    wiT = np.ascontiguousarray(w_in.transpose(0, 2, 1))
    woT = np.ascontiguousarray(w_out.transpose(0, 2, 1))

    def dirpack(tag):
        inw = f32(inputs["L0_attn_in_w_" + tag])
        xpj = f32(inputs["L0_attn_xproj_w_" + tag])
        dtw = f32(inputs["L0_attn_dt_w_" + tag])
        mout = f32(inputs["L0_attn_mout_w_" + tag])
        alog_ = f32(inputs["L0_attn_A_log_" + tag])
        Dd = f32(inputs["L0_attn_D_" + tag])
        dtbv = f32(inputs["L0_attn_dt_b_" + tag])
        cbb = f32(inputs["L0_attn_conv_b_" + tag])
        cww = f32(inputs["L0_attn_conv_w_" + tag])[:, 0, :]
        return (np.ascontiguousarray(inw.T), np.ascontiguousarray(xpj.T),
                np.ascontiguousarray(dtw.T), np.ascontiguousarray(mout.T),
                np.ascontiguousarray(alog_.reshape(IC, P, DS).transpose(1, 0, 2)
                                     .reshape(P, IC * DS)),
                np.ascontiguousarray(Dd.reshape(IC, P).T),
                np.ascontiguousarray(dtbv.reshape(IC, P).T),
                np.ascontiguousarray(cbb.reshape(IC, P).T),
                np.ascontiguousarray(cww.reshape(IC, P, 4).transpose(1, 0, 2)
                                     .reshape(P, IC * 4)))

    pf = dirpack("f")
    pb = dirpack("b")
    shared = dict(
        wgT=wgT, wiT=wiT, woT=woT,
        inwT=np.stack([pf[0], pb[0]]), xprojT=np.stack([pf[1], pb[1]]),
        dtwT=np.stack([pf[2], pb[2]]), moutT=np.stack([pf[3], pb[3]]),
        alog=np.stack([pf[4], pb[4]]), d_r=np.stack([pf[5], pb[5]]),
        dtb_r=np.stack([pf[6], pb[6]]), cb_r=np.stack([pf[7], pb[7]]),
        cw_r=np.stack([pf[8], pb[8]]),
        outwT=np.ascontiguousarray(f32(inputs["L0_attn_out_w"]).T),
        outb_r=np.ascontiguousarray(
            f32(inputs["L0_attn_out_b"]).reshape(FC, P).T),
        gam_r=np.ascontiguousarray(
            f32(inputs["L0_attn_gamma"]).reshape(FC, P).T),
        bet_r=np.ascontiguousarray(
            f32(inputs["L0_attn_beta"]).reshape(FC, P).T),
        ident=np.eye(P, dtype=np.float32),
        constv=np.broadcast_to(
            np.array([EPS, -0.5, 1.0, -1.0, 0.0, 0.5, 0.0, 0.0], np.float32),
            (P, 8)).copy(),
        ones1=np.ones((1, P), np.float32),
        onesk=np.ones((P, 1), np.float32),
    )

    in_maps = []
    for c in range(NCORES):
        m = dict(shared)
        m["x"] = np.ascontiguousarray(xs[2 * c:2 * c + 2].transpose(0, 2, 1))
        in_maps.append(m)

    if "nc" not in _NC_CACHE:
        _NC_CACHE["nc"] = build_program()
    res = run_bass_kernel_spmd(_NC_CACHE["nc"], in_maps,
                               core_ids=list(range(NCORES)))
    outs = [np.asarray(res.results[c]["out"], np.float32).transpose(0, 2, 1)
            for c in range(NCORES)]
    return np.ascontiguousarray(np.concatenate(outs, axis=0))


if __name__ == "__main__":
    nc = build_program()
    print("built")


# version 09 (final)
# speedup vs baseline: 1.0348x; 1.0348x over previous
import numpy as np
import concourse.bass as bass
import concourse.mybir as mybir
from concourse.bass_utils import run_bass_kernel_spmd

F32 = mybir.dt.float32
BF16 = mybir.dt.bfloat16
ALU = mybir.AluOpType
AF = mybir.ActivationFunctionType
AX = mybir.AxisListType

P = 128
D = 1024
FC = D // P
DI = 2048
IC = DI // P
HID = 4096
NE = 8
DS = 16
DTR = 64
L = 4096
NSEQ = 2
CH = 512
FCH = 1024
NCH = L // CH
NSC = L // FCH
TS = 256
NSUB = CH // TS
EPS = 1e-5
NCORES = 8

ENGINES = ("sync", "tensor", "vector", "scalar", "gpsimd")


def ts_(i, n):
    return slice(i * n, (i + 1) * n)


class Sem:
    __slots__ = ("h", "n", "name")

    def __init__(self, h, name):
        self.h = h
        self.n = 0
        self.name = name


class Rot:
    def __init__(self, banks):
        self.banks = banks
        self.i = 0
        self.freed = []

    def get(self):
        j = self.i % len(self.banks)
        w = []
        if self.i >= len(self.banks):
            w.append(self.freed[self.i - len(self.banks)])
        self.i += 1
        return self.banks[j], w

    def free(self, pair):
        self.freed.append(pair)


class MK:
    def __init__(self):
        self.nc = bass.Bass("TRN2", target_bir_lowering=False)
        self.prog = {e: [] for e in ENGINES}
        self.pending = {e: [] for e in ENGINES}
        self.floor = {e: {} for e in ENGINES}
        self._soff = (int(self.nc.sbuf_base) + 31) // 32 * 32
        self._ctxs = []
        self.sems = []

    def salloc(self, name, shape, dtype, at=None):
        itm = 2 if dtype == BF16 else 4
        nb = int(np.prod(shape[1:])) * itm
        off = self._soff if at is None else at
        t = self.nc.alloc_sbuf_tensor_at(name, list(shape), dtype, offset=off)
        if at is None:
            self._soff = off + ((nb + 31) // 32) * 32
        return t.ap()

    def sem(self, name):
        cm = self.nc.semaphore(name)
        h = cm.__enter__()
        self._ctxs.append(cm)
        s = Sem(h, name)
        self.sems.append(s)
        return s

    def op(self, eng, build, waits=(), inc=None, k=1):
        fl = self.floor[eng]
        best = {}
        for x in list(self.pending[eng]) + list(waits):
            if x is None or x[1] <= 0:
                continue
            s, v = x
            if v > best.get(s, 0):
                best[s] = v
        if self.pending[eng]:
            self.pending[eng] = []
        w = []
        for s, v in best.items():
            if v > fl.get(s, 0):
                fl[s] = v
                w.append((s, v))
        self.prog[eng].append((w, build, inc.h if inc is not None else None, k))
        if inc is not None:
            inc.n += k

    def dma(self, q, out, in_, waits=(), inc=None, accum=False):
        if accum:
            def b(e, out=out, in_=in_):
                return e.dma_start(out=out, in_=in_, accum_op=ALU.add)
        else:
            def b(e, out=out, in_=in_):
                return e.dma_start(out=out, in_=in_)
        self.op(q, b, waits, inc, 16)

    def fence(self, engines=ENGINES):
        snap = [(s, s.n) for s in self.sems if s.n > 0]
        for e in engines:
            self.pending[e] = list(snap)

    def emit(self):
        block = self.nc.Block()
        block.__enter__()
        self._ctxs.append(block)
        for e in ENGINES:
            lst = self.prog[e]

            def body(eng, lst=lst):
                for w, build, inch, k in lst:
                    for s, v in w:
                        eng.wait_ge(s.h, v)
                    ins = build(eng)
                    if inch is not None:
                        ins.then_inc(inch, k)

            getattr(block, e)(body)


class Stream:
    def __init__(self, mk, q, slots, name):
        self.mk = mk
        self.q = q
        self.slots = slots
        self.sp = mk.sem("stp_" + name)
        self.pi = 0
        self.done_pairs = []

    def put(self, in_ap, extra=()):
        j = self.pi % len(self.slots)
        w = list(extra)
        if self.pi >= len(self.slots):
            w.append(self.done_pairs[self.pi - len(self.slots)])
        self.mk.dma(self.q, self.slots[j], in_ap, waits=w, inc=self.sp)
        self.pi += 1
        return self.slots[j], (self.sp, 16 * self.pi)

    def done(self, pair):
        self.done_pairs.append(pair)


def build_program(dbg=False):
    mk = MK()
    nc = mk.nc

    def din(name, shape, dtype=F32):
        return nc.dram_tensor(name, list(shape), dtype, kind="ExternalInput").ap()

    def dint(name, shape, dtype):
        return nc.dram_tensor(name, list(shape), dtype, kind="Internal").ap()

    x_in = din("x", [NSEQ, D, L])
    wgT = din("wgT", [D, NE])
    wiT = din("wiT", [NE, D, HID])
    woT = din("woT", [NE, HID, D])
    inwT = din("inwT", [2, D, 2 * DI])
    xprojT = din("xprojT", [2, DI, 96])
    dtwT = din("dtwT", [2, DTR, DI])
    moutT = din("moutT", [2, DI, D])
    outwT = din("outwT", [2 * D, D])
    alog = din("alog", [2, P, IC * DS])
    d_in = din("d_r", [2, P, IC])
    dtb_in = din("dtb_r", [2, P, IC])
    cb_in = din("cb_r", [2, P, IC])
    cw_in = din("cw_r", [2, P, IC * 4])
    outb_in = din("outb_r", [P, FC])
    gam_in = din("gam_r", [P, FC])
    bet_in = din("bet_r", [P, FC])
    ident_in = din("ident", [P, P])
    ones1_in = din("ones1", [1, P])
    onesk_in = din("onesk", [P, 1])
    const_in = din("constv", [P, 8])

    out_d = nc.dram_tensor("out", [NSEQ, D, L], F32, kind="ExternalOutput").ap()

    wiB = dint("wiB", [NE, D, HID], BF16)
    woB = dint("woB", [NE, HID, D], BF16)
    inwB = dint("inwB", [2, D, 2 * DI], BF16)
    xprojB = dint("xprojB", [2, DI, 96], BF16)
    dtwB = dint("dtwB", [2, DTR, DI], BF16)
    moutB = dint("moutB", [2, DI, D], BF16)
    outwB = dint("outwB", [2 * D, D], BF16)

    dkind = "ExternalOutput" if dbg else "Internal"
    x1S = nc.dram_tensor("x1S", [NSEQ, D, L], F32, kind=dkind).ap()
    sS = nc.dram_tensor("sS", [NSEQ, D, L], F32, kind=dkind).ap()
    hS = nc.dram_tensor("hS", [NSEQ, D, L], BF16, kind=dkind).ap()
    bcS = dint("bcS", [32, CH], BF16)
    gS = dint("gS", [NE, FCH], BF16)

    A = mk.salloc
    identf = A("identf", [P, P], F32)
    constf = A("constf", [P, 8], F32)
    for _ci, _cv in enumerate((EPS, -0.5, 1.0, -1.0, 0.0, 0.5)):
        nc.const_aps.register(F32, _cv, constf[:, _ci:_ci + 1])
    ones1 = A("ones1s", [1, P], F32)
    oneskb = A("oneskb", [P, 16], BF16)
    wg_sb = A("wg_sb", [P, FC * NE], BF16)
    gam = A("gam", [P, FC], F32)
    bet = A("bet", [P, FC], F32)
    outb = A("outb", [P, FC], F32)
    Dv = A("Dv", [P, IC], F32)
    dtb = A("dtb", [P, IC], F32)
    cbv = A("cbv", [P, IC], F32)
    cwv = A("cwv", [P, IC * 4], F32)
    Aneg = A("Aneg", [P, IC * DS], F32)
    hstate = A("hstate", [P, IC * DS], F32)
    tails = A("tails", [P, IC * 4], BF16)
    xproj_sb = A("xproj_sb", [P, IC * 96], BF16)
    dtw_sb = A("dtw_sb", [DTR, DI], BF16)
    ow_stg = A("ow_stg", [P, 2 * FC * P], BF16)
    rows = A("rows", [1, 8 * 512], F32)
    m1p = A("m1p", [P, 8], F32)
    m2p = A("m2p", [P, 8], F32)
    dpk = A("dpk", [P, 8], F32)
    g1p = A("g1p", [P, 8], F32)
    g2p = A("g2p", [P, 8], F32)
    ltm8 = A("ltm8", [P, 64], F32)
    l2t8 = A("l2t8", [P, 64], F32)
    mk18 = A("mk18", [P, 64], F32)
    mk28 = A("mk28", [P, 64], F32)
    gtm = A("gtm", [P, 8], F32)
    logit_sb = A("logit_sb", [NE, FCH], F32)
    iw_stg = A("iw_stg", [P, 2 * FC * 512], BF16)
    wi_stg = A("wi_stg", [P, 2 * FC * 512], BF16)
    wo_stg = A("wo_stg", [P, 2 * 4 * 1024], BF16)
    mo_stg = A("mo_stg", [P, 2 * 512], BF16)
    gbc = A("gbc", [P, 2 * FCH], BF16)
    x1t = A("x1t", [P, 2 * CH], F32)
    sstg = A("sstg", [P, 2 * CH], F32)
    ystg = A("ystg", [P, 2 * 512], F32)
    xdblb = A("xdblb", [P, CH], BF16)

    R0 = mk._soff
    x_bf = A("x_bf", [P, FC * FCH], BF16, at=R0)
    mk._soff = R0 + FC * FCH * 2
    x_ln = A("x_ln", [P, FC * FCH], BF16)
    y_acc = A("y_acc", [P, FC * FCH], BF16)
    h_blk = A("h_blk", [P, FC * FCH], BF16)
    ygt = A("ygt", [P, 2 * 512], F32)
    ffn_end = mk._soff

    mk._soff = R0
    x1bf = A("x1bf", [P, FC * CH], BF16, at=R0)
    mk._soff = R0 + FC * CH * 2
    h_sb = A("h_sb", [P, FC * CH], BF16)
    z_off = mk._soff
    z_sb = A("z_sb", [P, IC * CH], BF16)
    xc = A("xc", [P, IC * CH], BF16)
    xib = A("xib", [P, 2 * 520], BF16)
    mo = A("mo", [P, FC * CH], BF16, at=z_off)
    delt = A("delt", [P, TS], F32)
    duv = A("duv", [P, TS], F32)
    a_s = A("a_s", [P, 4 * TS], F32)
    b_s = A("b_s", [P, 2 * TS], F32)
    h_out = A("h_out", [P, DS * TS], F32)
    Bbc = A("Bbc", [P, DS * TS], BF16)
    Cbc = A("Cbc", [P, DS * TS], BF16)
    b_all = A("b_all", [P, 8 * TS], F32)
    yscan = A("yscan", [P, CH], F32)
    ytmp = A("ytmp", [P, CH], F32)
    mam_end = mk._soff
    yg = A("yg", [P, IC * CH], BF16, at=R0)
    assert max(ffn_end, mam_end) <= int(nc.sbuf_top) - 64, (ffn_end, mam_end, nc.sbuf_top)

    ps = []
    for i in range(8):
        cm = nc.psum_tensor("psb%d" % i, [P, 512], F32)
        ps.append(cm.__enter__().ap())
        mk._ctxs.append(cm)

    s_pc = mk.sem("s_pc")
    s_cl = mk.sem("s_cl")
    s_xg = mk.sem("s_xg")
    s_hl = mk.sem("s_hl")
    s_ac = mk.sem("s_ac")
    s_st = mk.sem("s_st")
    s_bc = mk.sem("s_bc")
    s_pe = mk.sem("s_pe")
    s_dv = mk.sem("s_dv")
    s_sc = mk.sem("s_sc")
    s_po = mk.sem("s_po")

    def dve(build, waits=(), edge=False):
        w = list(waits)
        if edge:
            w.append((s_dv, s_dv.n))
        mk.op("vector", build, w, s_dv, 1)

    def act(build, waits=(), edge=False):
        w = list(waits)
        if edge:
            w.append((s_sc, s_sc.n))
        mk.op("scalar", build, w, s_sc, 1)

    def pool(build, waits=(), edge=False):
        w = list(waits)
        if edge:
            w.append((s_po, s_po.n))
        mk.op("gpsimd", build, w, s_po, 1)

    def mmgroup(lhsT_aps, rhs_aps, out_ap, waits=(), inc=None):
        n = len(lhsT_aps)
        for i, (lt, rh) in enumerate(zip(lhsT_aps, rhs_aps)):
            w = waits if i == 0 else ()
            ic = inc if i == n - 1 else None

            def b(e, lt=lt, rh=rh, st=(i == 0), sp=(i == n - 1), o=out_ap):
                return e.matmul(o, lt, rh, start=st, stop=sp)
            mk.op("tensor", b, w, ic, 1)

    rotA = Rot([ps[0], ps[1]])
    rotB = Rot([ps[2], ps[3]])

    st_iw = Stream(mk, "sync",
                   [iw_stg[:, ts_(j, FC * 512)].rearrange("p (k m) -> p k m", k=FC)
                    for j in range(2)], "iw")
    st_wi = Stream(mk, "sync",
                   [wi_stg[:, ts_(j, FC * 512)].rearrange("p (k m) -> p k m", k=FC)
                    for j in range(2)], "wi")
    st_wo = Stream(mk, "sync",
                   [wo_stg[:, ts_(j, 4 * 1024)].rearrange("p (k m) -> p k m", k=4)
                    for j in range(2)], "wo")
    st_mo = Stream(mk, "sync", [mo_stg[:, ts_(j, 512)] for j in range(2)], "mo")
    st_gb = Stream(mk, "sync", [gbc[:, ts_(j, FCH)] for j in range(2)], "gb")
    st_x1 = Stream(mk, "sync", [x1t[:, ts_(j, CH)] for j in range(2)], "x1")
    st_ow = Stream(mk, "sync",
                   [ow_stg[:, ts_(j, FC * P)].rearrange("p (k m) -> p k m", k=FC)
                    for j in range(2)], "ow")

    ss_prev = [None, None]
    ys_prev = [None, None]
    ab_free = [0, 0, 0, 0]

    for dst, src in ((identf, ident_in), (constf, const_in), (ones1, ones1_in),
                     (gam, gam_in), (bet, bet_in), (outb, outb_in)):
        mk.dma("sync", dst, src, inc=s_cl)
    mk.dma("gpsimd", oneskb[:, 0:1], onesk_in, inc=s_pc)
    mk.dma("gpsimd", wg_sb.rearrange("p (k e) -> p k e", k=FC),
           wgT.rearrange("(k p) e -> p k e", p=P), inc=s_pc)
    for e_i in range(NE):
        mk.dma("gpsimd", wiB[e_i], wiT[e_i], inc=s_pc)
        mk.dma("gpsimd", woB[e_i], woT[e_i], inc=s_pc)
    for d_ in range(2):
        mk.dma("gpsimd", inwB[d_], inwT[d_], inc=s_pc)
        mk.dma("gpsimd", xprojB[d_], xprojT[d_], inc=s_pc)
        mk.dma("gpsimd", dtwB[d_], dtwT[d_], inc=s_pc)
        mk.dma("gpsimd", moutB[d_], moutT[d_], inc=s_pc)
    mk.dma("gpsimd", outwB, outwT, inc=s_pc)
    PC0 = s_pc.n
    CL0 = s_cl.n

    def row(off, n):
        return rows[0:1, off:off + n]

    def stats_mms(xv, tcount, sqv, xwait):
        act(lambda e: e.activation(out=sqv, in_=xv, func=AF.Square),
            waits=[xwait], edge=True)
        sqsc = (s_sc, s_sc.n)
        nh = tcount // 512
        banks = [ps[4], ps[5], ps[6], ps[7]]
        bi = 0
        for q, (src, swt, ro) in enumerate(((xv, xwait, 0), (sqv, sqsc, 1024))):
            for h in range(nh):
                bank = banks[bi]
                bi += 1
                w = [swt, (s_sc, s_sc.n)]
                mmgroup([oneskb[:, 0:1]] * FC,
                        [src[:, c, ts_(h, 512)] for c in range(FC)],
                        bank[0:1, :], waits=w, inc=s_pe)
                pv = (s_pe, s_pe.n)
                act(lambda e, bank=bank, ro=ro, h=h:
                    e.activation(out=row(ro + h * 512, 512), in_=bank[0:1, :],
                                 func=AF.Copy), waits=[pv])
        return (s_sc, s_sc.n)

    def ln_rows(tcount, attn, swait):
        m = row(0, tcount)
        v = row(1024, tcount)
        ta = row(2048, tcount)
        tb = row(3072, tcount)
        c = 1.0 / D
        dve(lambda e: e.tensor_scalar_mul(out=m, in0=m, scalar1=c),
            waits=[swait], edge=True)
        dve(lambda e: e.tensor_tensor(out=ta, in0=m, in1=m, op=ALU.mult),
            edge=True)
        dve(lambda e: e.scalar_tensor_tensor(out=v, in0=v, scalar=c, in1=ta,
                                             op0=ALU.mult, op1=ALU.subtract),
            edge=True)
        vdv = (s_dv, s_dv.n)
        act(lambda e: e.activation(out=ta, in_=v, func=AF.Ln, bias=EPS),
            waits=[vdv], edge=True)
        act(lambda e: e.activation(out=tb, in_=ta, func=AF.Exp, scale=-0.5),
            edge=True)
        r1sc = (s_sc, s_sc.n)
        if not attn:
            Arow, Brow = tb, ta
            dve(lambda e, Arow=Arow, Brow=Brow: e.scalar_tensor_tensor(
                out=Brow, in0=m, scalar=-1.0, in1=Arow, op0=ALU.mult,
                op1=ALU.mult), waits=[r1sc], edge=True)
        else:
            tv = row(512, tcount)
            dve(lambda e: e.tensor_scalar_add(out=tv, in0=v, scalar1=EPS),
                waits=[r1sc], edge=True)
            dve(lambda e: e.reciprocal(out=ta, in_=tv), edge=True)
            dve(lambda e: e.tensor_tensor(out=tv, in0=v, in1=ta, op=ALU.mult),
                edge=True)
            v2dv = (s_dv, s_dv.n)
            act(lambda e: e.activation(out=ta, in_=tv, func=AF.Ln, bias=EPS),
                waits=[v2dv], edge=True)
            act(lambda e: e.activation(out=tv, in_=ta, func=AF.Exp, scale=-0.5),
                edge=True)
            r2sc = (s_sc, s_sc.n)
            Arow = row(2048, tcount)
            Brow = row(3584, tcount)
            dve(lambda e, Arow=Arow: e.tensor_tensor(
                out=Arow, in0=tb, in1=tv, op=ALU.mult), waits=[r2sc], edge=True)
            dve(lambda e, Arow=Arow, Brow=Brow: e.scalar_tensor_tensor(
                out=Brow, in0=m, scalar=-1.0, in1=Arow, op0=ALU.mult,
                op1=ALU.mult), edge=True)
        return Arow, Brow, (s_dv, s_dv.n), (s_sc, s_sc.n)

    def bcast_rows(Arow, Brow, tcount, wdv, wsc):
        out = []
        nh = tcount // 512
        for h in range(nh):
            bA = ps[4 + 2 * h]
            bB = ps[5 + 2 * h]
            mmgroup([ones1], [Arow[0:1, ts_(h, 512)]], bA[:, 0:512],
                    waits=[wdv, wsc, (s_sc, s_sc.n)], inc=s_pe)
            mmgroup([ones1], [Brow[0:1, ts_(h, 512)]], bB[:, 0:512], inc=s_pe)
            out.append((bA, bB))
        return out, (s_pe, s_pe.n)

    def ffn_phase(p2, seq, sc):
        t0 = sc * FCH
        trg = slice(t0, t0 + FCH)
        src_d = out_d if p2 else x_in
        acc_d = out_d if p2 else x1S
        xv = x_bf.rearrange("p (c t) -> p c t", c=FC)
        lv = x_ln.rearrange("p (c t) -> p c t", c=FC)
        hv = h_blk.rearrange("p (c t) -> p c t", c=FC)
        yv = y_acc.rearrange("p (c t) -> p c t", c=FC)

        mk.fence()
        if not p2:
            mk.dma("sync", x1S[seq][:, trg], x_in[seq][:, trg], inc=s_st)
        mk.dma("gpsimd", xv,
               src_d[seq].rearrange("(c p) t -> p c t", p=P)[:, :, trg],
               waits=[(s_pc, PC0)], inc=s_xg)
        xw = (s_xg, s_xg.n)

        sw = stats_mms(xv, FCH, hv, xw)
        Arow, Brow, wdv, wsc = ln_rows(FCH, False, sw)
        bc, wpe = bcast_rows(Arow, Brow, FCH, wdv, wsc)
        for h in range(2):
            bA, bB = bc[h]
            for c in range(FC):
                dve(lambda e, c=c, h=h, bA=bA: e.tensor_tensor(
                    out=ygt[:, 0:512], in0=xv[:, c, ts_(h, 512)], in1=bA[:, 0:512],
                    op=ALU.mult),
                    waits=[wpe, xw] if (c == 0 and h == 0) else (), edge=True)
                dve(lambda e, c=c, h=h, bB=bB: e.tensor_tensor(
                    out=lv[:, c, ts_(h, 512)], in0=ygt[:, 0:512], in1=bB[:, 0:512],
                    op=ALU.add), edge=True)
        xln_dv = (s_dv, s_dv.n)

        for h in range(2):
            mmgroup([wg_sb[:, ts_(c, NE)] for c in range(FC)],
                    [lv[:, c, ts_(h, 512)] for c in range(FC)],
                    ps[6][0:NE, :], waits=[xln_dv, (s_sc, s_sc.n)], inc=s_pe)
            pv = (s_pe, s_pe.n)
            act(lambda e, h=h: e.activation(out=logit_sb[:, ts_(h, 512)],
                                            in_=ps[6][0:NE, :], func=AF.Copy),
                waits=[pv])
        lg_sc = (s_sc, s_sc.n)
        for tt in range(8):
            mk.op("tensor", lambda e, tt=tt: e.transpose(
                ps[7][:, 0:NE], logit_sb[0:NE, ts_(tt, P)], identf[0:NE, 0:NE]),
                waits=[lg_sc] if tt == 0 else [(s_sc, s_sc.n)], inc=s_pe)
            pv = (s_pe, s_pe.n)
            act(lambda e, tt=tt: e.activation(out=ltm8[:, ts_(tt, NE)],
                                              in_=ps[7][:, 0:NE], func=AF.Copy),
                waits=[pv])
        ev_sc = (s_sc, s_sc.n)
        for tt in range(8):
            dve(lambda e, tt=tt: e.tensor_reduce(
                out=m1p[:, tt:tt + 1], in_=ltm8[:, ts_(tt, NE)], axis=AX.X,
                op=ALU.max), waits=[ev_sc] if tt == 0 else ())
        dve(lambda e: e.tensor_copy(out=gtm, in_=m1p), edge=True)
        for tt in range(8):
            dve(lambda e, tt=tt: e.tensor_scalar(
                out=mk18[:, ts_(tt, NE)], in0=ltm8[:, ts_(tt, NE)],
                scalar1=m1p[:, tt:tt + 1], scalar2=None, op0=ALU.is_equal))
            dve(lambda e, tt=tt: e.scalar_tensor_tensor(
                out=l2t8[:, ts_(tt, NE)], in0=mk18[:, ts_(tt, NE)], scalar=-1e30,
                in1=ltm8[:, ts_(tt, NE)], op0=ALU.mult, op1=ALU.add), edge=True)
        for tt in range(8):
            dve(lambda e, tt=tt: e.tensor_reduce(
                out=m2p[:, tt:tt + 1], in_=l2t8[:, ts_(tt, NE)], axis=AX.X,
                op=ALU.max), edge=(tt == 0))
        dve(lambda e: e.tensor_tensor(out=dpk, in0=m1p, in1=m2p,
                                      op=ALU.subtract), edge=True)
        ddv = (s_dv, s_dv.n)
        act(lambda e: e.activation(out=g1p, in_=dpk, func=AF.Sigmoid),
            waits=[ddv], edge=True)
        sg = (s_sc, s_sc.n)
        dve(lambda e: e.tensor_scalar(out=g2p, in0=g1p, scalar1=-1.0,
                                      scalar2=1.0, op0=ALU.mult, op1=ALU.add),
            waits=[sg], edge=True)
        dve(lambda e: e.tensor_copy(out=gtm, in_=g2p), edge=True)
        for tt in range(8):
            dve(lambda e, tt=tt: e.tensor_scalar(
                out=mk28[:, ts_(tt, NE)], in0=l2t8[:, ts_(tt, NE)],
                scalar1=m2p[:, tt:tt + 1], scalar2=None, op0=ALU.is_equal))
            dve(lambda e, tt=tt: e.tensor_scalar_mul(
                out=mk18[:, ts_(tt, NE)], in0=mk18[:, ts_(tt, NE)],
                scalar1=g1p[:, tt:tt + 1]))
            dve(lambda e, tt=tt: e.tensor_scalar_mul(
                out=mk28[:, ts_(tt, NE)], in0=mk28[:, ts_(tt, NE)],
                scalar1=g2p[:, tt:tt + 1]), edge=True)
            dve(lambda e, tt=tt: e.tensor_tensor(
                out=gtm, in0=mk18[:, ts_(tt, NE)], in1=mk28[:, ts_(tt, NE)],
                op=ALU.add), edge=True)
            gdv = (s_dv, s_dv.n)
            mk.op("tensor", lambda e, tt=tt: e.transpose(
                ps[7][0:NE, 0:P], gtm, identf),
                waits=[gdv, (s_sc, s_sc.n)], inc=s_pe)
            pv = (s_pe, s_pe.n)
            act(lambda e, tt=tt: e.activation(
                out=logit_sb[0:NE, ts_(tt, P)], in_=ps[7][0:NE, 0:P],
                func=AF.Copy), waits=[pv])
        g_sc = (s_sc, s_sc.n)
        mk.dma("gpsimd", gS, logit_sb, waits=[g_sc], inc=s_xg)
        g_xg = (s_xg, s_xg.n)

        for e_i in range(NE):
            gslot, gpv = st_gb.put(gS[e_i:e_i + 1, :].to_broadcast([P, FCH]),
                                   extra=[g_xg])
            for hb in range(4):
                wi_aps = []
                for sh in range(2):
                    mc = slice(hb * 1024 + sh * 512, hb * 1024 + (sh + 1) * 512)
                    wi_aps.append(st_wi.put(
                        wiB[e_i].rearrange("(k p) m -> p k m", p=P)[:, :, mc],
                        extra=[(s_pc, PC0)]))
                wo_aps = []
                for sh in range(2):
                    k0 = (hb * 1024 + sh * 512) // P
                    wo_aps.append(st_wo.put(
                        woB[e_i].rearrange("(k p) m -> p k m", p=P)[:, k0:k0 + 4, :],
                        extra=[(s_pc, PC0)]))
                for mt in range(8):
                    sap, spv = wi_aps[mt // 4]
                    lhs = sap[:, :, ts_(mt % 4, P)]
                    for nh in range(2):
                        bank, bw = rotA.get()
                        w = [spv] + bw
                        if mt == 0 and nh == 0:
                            w.append(xln_dv)
                        mmgroup([lhs[:, c, :] for c in range(FC)],
                                [lv[:, c, ts_(nh, 512)] for c in range(FC)],
                                bank, waits=w, inc=s_pe)
                        pv = (s_pe, s_pe.n)
                        act(lambda e, bank=bank, mt=mt, nh=nh: e.activation(
                            out=hv[:, mt, ts_(nh, 512)], in_=bank, func=AF.Copy),
                            waits=[pv])
                        rotA.free((s_sc, s_sc.n))
                st_wi.done((s_pe, s_pe.n))
                st_wi.done((s_pe, s_pe.n))
                h_sc = (s_sc, s_sc.n)
                for m in range(FC):
                    for nh in range(2):
                        bank, bw = rotB.get()
                        lhs_list = []
                        rhs_list = []
                        for sh in range(2):
                            sap2, spv2 = wo_aps[sh]
                            for kc in range(4):
                                lhs_list.append(sap2[:, kc, ts_(m, P)])
                                rhs_list.append(hv[:, sh * 4 + kc, ts_(nh, 512)])
                        w = bw + ([wo_aps[1][1], h_sc]
                                  if (m == 0 and nh == 0) else [])
                        mmgroup(lhs_list, rhs_list, bank, waits=w, inc=s_pe)
                        pv = (s_pe, s_pe.n)
                        dve(lambda e, bank=bank, nh=nh, gslot=gslot:
                            e.tensor_tensor(out=ygt[:, ts_(nh, 512)], in0=bank,
                                            in1=gslot[:, ts_(nh, 512)],
                                            op=ALU.mult), waits=[pv, gpv])
                        rotB.free((s_dv, s_dv.n))
                        if e_i == 0 and hb == 0:
                            def bb(e, m=m, nh=nh):
                                return e.tensor_scalar_mul(
                                    out=yv[:, m, ts_(nh, 512)],
                                    in0=ygt[:, ts_(nh, 512)], scalar1=0.5)
                        else:
                            def bb(e, m=m, nh=nh):
                                return e.scalar_tensor_tensor(
                                    out=yv[:, m, ts_(nh, 512)],
                                    in0=ygt[:, ts_(nh, 512)], scalar=0.5,
                                    in1=yv[:, m, ts_(nh, 512)],
                                    op0=ALU.mult, op1=ALU.add)
                        dve(bb, edge=True)
                st_wo.done((s_pe, s_pe.n))
                st_wo.done((s_pe, s_pe.n))
            st_gb.done((s_dv, s_dv.n))
        yfin = (s_dv, s_dv.n)
        for m in range(FC):
            for nh in range(2):
                j = (m * 2 + nh) % 2
                w = [yfin, ys_prev[j]]
                dve(lambda e, m=m, nh=nh, j=j: e.tensor_copy(
                    out=ystg[:, ts_(j, 512)], in_=yv[:, m, ts_(nh, 512)]),
                    waits=w)
                cdv = (s_dv, s_dv.n)
                mk.dma("gpsimd",
                       acc_d[seq][ts_(m, P), t0 + nh * 512:t0 + (nh + 1) * 512],
                       ystg[:, ts_(j, 512)], waits=[cdv, (s_st, s_st.n)],
                       inc=s_ac, accum=True)
                ys_prev[j] = (s_ac, s_ac.n)

    def mamba_phase(p2, seq, k, first_of_seq):
        dir_ = 1 if p2 else 0
        rev = p2
        t0 = k * CH
        trg = slice(t0, t0 + CH)
        hvv = h_sb.rearrange("p (c t) -> p c t", c=FC)
        xcv = xc.rearrange("p (c t) -> p c t", c=IC)
        zvv = z_sb.rearrange("p (c t) -> p c t", c=IC)
        ygv = yg.rearrange("p (c t) -> p c t", c=IC)
        mov = mo.rearrange("p (c t) -> p c t", c=FC)
        hov = h_out.rearrange("p (s t) -> p s t", s=DS)
        Bv = Bbc.rearrange("p (s t) -> p s t", s=DS)
        Cv = Cbc.rearrange("p (s t) -> p s t", s=DS)

        mk.fence()
        if first_of_seq:
            dve(lambda e: e.memset(hstate, 0.0))
            dve(lambda e: e.memset(tails.bitcast(F32), 0.0))
            dve(lambda e: e.tensor_copy(out=gtm, in_=m1p), edge=True)
        init_dv = (s_dv, s_dv.n)

        if not p2:
            x1v = x1bf.rearrange("p (c t) -> p c t", c=FC)
            mk.dma("gpsimd", x1v,
                   x1S[seq].rearrange("(c p) t -> p c t", p=P)[:, :, trg],
                   waits=[(s_ac, s_ac.n)], inc=s_xg)
            xw = (s_xg, s_xg.n)
            sw = stats_mms(x1v, CH, zvv[:, 0:FC, :], xw)
            Arow, Brow, wdv, wsc = ln_rows(CH, True, sw)
            bc, wpe = bcast_rows(Arow, Brow, CH, wdv, wsc)
            bA, bB = bc[0]
            for c in range(FC):
                dve(lambda e, c=c, bA=bA: e.tensor_tensor(
                    out=yscan, in0=x1v[:, c, :], in1=bA[:, 0:512], op=ALU.mult),
                    waits=[wpe, xw] if c == 0 else (), edge=True)
                dve(lambda e, bB=bB: e.tensor_tensor(
                    out=yscan, in0=yscan, in1=bB[:, 0:512], op=ALU.add),
                    edge=True)
                dve(lambda e, c=c: e.tensor_scalar(
                    out=hvv[:, c, :], in0=yscan, scalar1=gam[:, c:c + 1],
                    scalar2=bet[:, c:c + 1], op0=ALU.mult, op1=ALU.add),
                    edge=True)
            h_rdy = (s_dv, s_dv.n)
            dve(lambda e: e.scalar_tensor_tensor(
                out=rows[0:1, 1536:2048], in0=rows[0:1, 0:512], scalar=-1.0,
                in1=rows[0:1, 3072:3584], op0=ALU.mult, op1=ALU.mult),
                edge=True)
            b1_dv = (s_dv, s_dv.n)
            mk.dma("sync", hS[seq].rearrange("(c p) t -> p c t", p=P)[:, :, trg],
                   hvv, waits=[h_rdy], inc=s_st)
            h_saved = (s_st, s_st.n)
        else:
            mk.dma("sync", hvv,
                   hS[seq].rearrange("(c p) t -> p c t", p=P)[:, :, trg],
                   inc=s_hl)
            h_rdy = (s_hl, s_hl.n)
            h_saved = None

        xi_prev = [None, None]
        last_silu = None
        for si in range(8):
            mc = ts_(si, 512)
            sap, spv = st_iw.put(
                inwB[dir_].rearrange("(k p) m -> p k m", p=P)[:, :, mc],
                extra=[(s_pc, PC0)])
            for mt in range(4):
                gi = si * 4 + mt
                lhs = sap[:, :, ts_(mt, P)]
                bank, bw = rotA.get()
                mmgroup([lhs[:, c, :] for c in range(FC)],
                        [hvv[:, c, :] for c in range(FC)],
                        bank, waits=[spv, h_rdy] + bw, inc=s_pe)
                pv = (s_pe, s_pe.n)
                if gi < IC:
                    dc = gi
                    j = dc % 2
                    dst = (xib[:, j * 520:j * 520 + 512] if rev
                           else xib[:, j * 520 + 3:j * 520 + 515])
                    dve(lambda e, bank=bank, dst=dst: e.tensor_copy(
                        out=dst, in_=bank), waits=[pv, xi_prev[j]])
                    rotA.free((s_dv, s_dv.n))
                    xidv = (s_dv, s_dv.n)
                    if rev:
                        pool(lambda e, j=j, dc=dc: e.tensor_copy(
                            out=xib[:, j * 520 + 512:j * 520 + 515],
                            in_=tails[:, dc * 4:dc * 4 + 3]),
                            waits=[xidv, init_dv], edge=True)
                        offs = [3, 2, 1, 0]
                    else:
                        pool(lambda e, j=j, dc=dc: e.tensor_copy(
                            out=xib[:, j * 520:j * 520 + 3],
                            in_=tails[:, dc * 4:dc * 4 + 3]),
                            waits=[xidv, init_dv], edge=True)
                        offs = [0, 1, 2, 3]
                    for ki in range(4):
                        o = offs[ki]
                        wv = cwv[:, dc * 4 + ki:dc * 4 + ki + 1]
                        if ki == 0:
                            pool(lambda e, j=j, o=o, wv=wv: e.tensor_tensor(
                                out=ytmp, in0=xib[:, j * 520 + o:j * 520 + o + 512],
                                in1=wv.to_broadcast([P, CH]), op=ALU.mult),
                                waits=[last_silu], edge=True)
                        else:
                            pool(lambda e, j=j, o=o, wv=wv: e.tensor_tensor(
                                out=yscan, in0=xib[:, j * 520 + o:j * 520 + o + 512],
                                in1=wv.to_broadcast([P, CH]), op=ALU.mult),
                                edge=True)
                            pool(lambda e: e.tensor_tensor(
                                out=ytmp, in0=ytmp, in1=yscan, op=ALU.add),
                                edge=True)
                    if rev:
                        pool(lambda e, j=j, dc=dc: e.tensor_copy(
                            out=tails[:, dc * 4:dc * 4 + 3],
                            in_=xib[:, j * 520:j * 520 + 3]), edge=True)
                    else:
                        pool(lambda e, j=j, dc=dc: e.tensor_copy(
                            out=tails[:, dc * 4:dc * 4 + 3],
                            in_=xib[:, j * 520 + 512:j * 520 + 515]), edge=True)
                    xi_prev[j] = (s_po, s_po.n)
                    podv = (s_po, s_po.n)
                    act(lambda e, dc=dc: e.activation(
                        out=xcv[:, dc, :], in_=ytmp, func=AF.Silu,
                        bias=cbv[:, dc:dc + 1]), waits=[podv])
                    last_silu = (s_sc, s_sc.n)
                else:
                    zc = gi - IC
                    act(lambda e, bank=bank, zc=zc: e.activation(
                        out=zvv[:, zc, :], in_=bank, func=AF.Silu), waits=[pv])
                    rotA.free((s_sc, s_sc.n))
            st_iw.done((s_pe, s_pe.n))
        xc_sc = (s_sc, s_sc.n)

        mmgroup([xproj_sb[:, ts_(c, 96)] for c in range(IC)],
                [xcv[:, c, :] for c in range(IC)],
                ps[5][0:96, :], waits=[xc_sc, (s_cl, CL0), (s_dv, s_dv.n)],
                inc=s_pe)
        pv = (s_pe, s_pe.n)
        dve(lambda e: e.tensor_copy(out=xdblb[0:96, :], in_=ps[5][0:96, :]),
            waits=[pv])
        xd_dv = (s_dv, s_dv.n)
        mk.dma("sync", bcS, xdblb[64:96, :],
               waits=[xd_dv, (s_bc, s_bc.n)], inc=s_st)
        bw0 = (s_st, s_st.n)

        subs = list(range(NSUB)) if not rev else list(range(NSUB - 1, -1, -1))
        ball = b_all.rearrange("p (s t) -> p s t", s=8)
        prodq_prev = [None] * 4
        sdvq_prev = [None] * 4
        bq2_prev = None
        yg_prev = None
        for isub, sub in enumerate(subs):
            tsl = ts_(sub, TS)
            bcw = [bw0]
            if isub > 0:
                bcw.append((s_dv, s_dv.n))
                bcw.append((s_po, s_po.n))
            mk.dma("sync", Bv,
                   bcS[0:16, tsl].unsqueeze(0).to_broadcast([P, DS, TS]),
                   waits=bcw, inc=s_bc)
            mk.dma("sync", Cv,
                   bcS[16:32, tsl].unsqueeze(0).to_broadcast([P, DS, TS]),
                   waits=[bw0] + ([(s_po, s_po.n)] if isub else []), inc=s_bc)
            bc_rdy = (s_bc, s_bc.n)
            dt_pv = []
            for dg in range(IC):
                mmgroup([dtw_sb[:, ts_(dg, P)]], [xdblb[0:DTR, tsl]],
                        ps[dg // 2][:, ts_(dg % 2, TS)],
                        waits=([xd_dv, (s_dv, s_dv.n), (s_sc, s_sc.n)]
                               if dg == 0 else []), inc=s_pe)
                dt_pv.append((s_pe, s_pe.n))
            for dg in range(IC):
                dsl = ps[dg // 2][:, ts_(dg % 2, TS)]
                dve(lambda e, dsl=dsl, dg=dg: e.tensor_scalar(
                    out=ytmp[:, 0:TS], in0=dsl,
                    scalar1=dtb[:, dg:dg + 1], scalar2=None, op0=ALU.add),
                    waits=[dt_pv[dg]], edge=True)
                dve(lambda e, dsl=dsl, dg=dg: e.tensor_scalar(
                    out=delt, in0=dsl, scalar1=dtb[:, dg:dg + 1],
                    scalar2=0.0, op0=ALU.add, op1=ALU.max))
                ac2 = (s_dv, s_dv.n)
                dve(lambda e: e.scalar_tensor_tensor(
                    out=ytmp[:, 0:TS], in0=delt, scalar=-2.0, in1=ytmp[:, 0:TS],
                    op0=ALU.mult, op1=ALU.add), waits=[ac2], edge=True)
                nudv = (s_dv, s_dv.n)
                act(lambda e: e.activation(out=ytmp[:, TS:2 * TS],
                                           in_=ytmp[:, 0:TS], func=AF.Exp),
                    waits=[nudv], edge=True)
                act(lambda e: e.activation(out=ytmp[:, 0:TS],
                                           in_=ytmp[:, TS:2 * TS], func=AF.Ln,
                                           bias=1.0), edge=True)
                spsc = (s_sc, s_sc.n)
                dve(lambda e: e.tensor_tensor(out=delt, in0=delt,
                                              in1=ytmp[:, 0:TS], op=ALU.add),
                    waits=[spsc], edge=True)
                dve(lambda e, dg=dg, tsl=tsl: e.tensor_tensor(
                    out=duv, in0=delt, in1=xcv[:, dg, tsl], op=ALU.mult),
                    waits=[bq2_prev], edge=True)
                du_dv = (s_dv, s_dv.n)
                del_dv = du_dv
                bq = [None] * 4
                for qb in range(2):
                    pool(lambda e, qb=qb: e.tensor_tensor(
                        out=ball[:, 4 * qb:4 * qb + 4, :],
                        in0=duv.unsqueeze(1).to_broadcast([P, 4, TS]),
                        in1=Bv[:, 4 * qb:4 * qb + 4, :], op=ALU.mult),
                        waits=[du_dv, sdvq_prev[qb + 2]]
                        + ([bc_rdy] if qb == 0 and dg == 0 else []))
                    bq[qb] = (s_po, s_po.n)
                saveq = []
                idx = 0 if rev else TS - 1
                for s_i in range(DS):
                    jj = s_i % 4
                    lane = (s_i // 4 % 2) * 4 + s_i % 4
                    asl = a_s[:, ts_(jj, TS)]
                    act(lambda e, dg=dg, s_i=s_i, asl=asl: e.activation(
                        out=asl, in_=delt, func=AF.Exp,
                        scale=Aneg[:, dg * DS + s_i:dg * DS + s_i + 1]),
                        waits=[del_dv, (s_dv, ab_free[jj])])
                    a_sc = (s_sc, s_sc.n)
                    if rev:
                        def sb(e, s_i=s_i, asl=asl, dg=dg, lane=lane):
                            return e.tensor_tensor_scan(
                                out=hov[:, s_i, ::-1], data0=asl[:, ::-1],
                                data1=ball[:, lane, ::-1],
                                initial=hstate[:, dg * DS + s_i:dg * DS + s_i + 1],
                                op0=ALU.mult, op1=ALU.add)
                    else:
                        def sb(e, s_i=s_i, asl=asl, dg=dg, lane=lane):
                            return e.tensor_tensor_scan(
                                out=hov[:, s_i, :], data0=asl,
                                data1=ball[:, lane, :],
                                initial=hstate[:, dg * DS + s_i:dg * DS + s_i + 1],
                                op0=ALU.mult, op1=ALU.add)
                    dve(sb, waits=[a_sc, bq[s_i // 4], prodq_prev[s_i // 4]])
                    ab_free[jj] = s_dv.n
                    sdvq_prev[s_i // 4] = (s_dv, s_dv.n)
                    if s_i % 4 == 3:
                        q = s_i // 4
                        dve(lambda e, dg=dg, q=q, idx=idx: e.tensor_copy(
                            out=hstate[:, dg * DS + 4 * q:dg * DS + 4 * q + 4],
                            in_=hov[:, 4 * q:4 * q + 4, idx]), edge=True)
                        saveq.append((s_dv, s_dv.n))
                        pool(lambda e, q=q: e.tensor_tensor(
                            out=h_out[:, 4 * q * TS:(4 * q + 4) * TS],
                            in0=h_out[:, 4 * q * TS:(4 * q + 4) * TS],
                            in1=Cbc[:, 4 * q * TS:(4 * q + 4) * TS],
                            op=ALU.mult),
                            waits=[saveq[-1]]
                            + ([bc_rdy] if q == 0 and dg == 0 else []))
                        prodq_prev[q] = (s_po, s_po.n)
                        if q == 0:
                            pool(lambda e: e.tensor_tensor(
                                out=ball[:, 0:4, :],
                                in0=duv.unsqueeze(1).to_broadcast([P, 4, TS]),
                                in1=Bv[:, 8:12, :], op=ALU.mult),
                                waits=[sdvq_prev[0]])
                            bq[2] = (s_po, s_po.n)
                        elif q == 1:
                            dve(lambda e: e.tensor_tensor(
                                out=ball[:, 4:8, :],
                                in0=duv.unsqueeze(1).to_broadcast([P, 4, TS]),
                                in1=Bv[:, 12:16, :], op=ALU.mult),
                                waits=[bc_rdy] if dg == 0 else (), edge=True)
                            bq[3] = (s_dv, s_dv.n)
                bq2_prev = bq[2]
                parts = [yscan[:, tsl], b_s[:, 0:TS], b_s[:, TS:2 * TS], duv]
                for q in range(4):
                    dve(lambda e, q=q, pt=parts[q]: e.tensor_reduce(
                        out=pt.unsqueeze(2),
                        in_=hov[:, 4 * q:4 * q + 4, :].transpose([0, 2, 1]),
                        axis=AX.X, op=ALU.add), waits=[prodq_prev[q]])
                dve(lambda e, tsl=tsl: e.tensor_tensor(
                    out=yscan[:, tsl], in0=yscan[:, tsl], in1=b_s[:, 0:TS],
                    op=ALU.add), edge=True)
                dve(lambda e: e.tensor_tensor(
                    out=b_s[:, TS:2 * TS], in0=b_s[:, TS:2 * TS], in1=duv,
                    op=ALU.add), edge=True)
                dve(lambda e, tsl=tsl: e.tensor_tensor(
                    out=yscan[:, tsl], in0=yscan[:, tsl],
                    in1=b_s[:, TS:2 * TS], op=ALU.add), edge=True)
                dve(lambda e, dg=dg, tsl=tsl: e.scalar_tensor_tensor(
                    out=ytmp[:, 0:TS], in0=xcv[:, dg, tsl],
                    scalar=Dv[:, dg:dg + 1], in1=yscan[:, tsl], op0=ALU.mult,
                    op1=ALU.add), edge=True)
                ygw = [h_saved] if (dg == 0 and isub == 0 and not p2) else []
                dve(lambda e, dg=dg, tsl=tsl: e.tensor_tensor(
                    out=ygv[:, dg, tsl], in0=ytmp[:, 0:TS], in1=zvv[:, dg, tsl],
                    op=ALU.mult), waits=ygw, edge=True)
                yg_prev = (s_dv, s_dv.n)
        yg_dv = (s_dv, s_dv.n)

        banks = [ps[0], ps[1], ps[2], ps[3]]
        for mh in range(2):
            first_w = [yg_dv, (s_sc, s_sc.n), (s_dv, s_dv.n)]
            for kdg in range(IC):
                sap, spv = st_mo.put(
                    moutB[dir_].rearrange("(k p) m -> p k m", p=P)
                    [:, kdg, ts_(mh, 512)], extra=[(s_pc, PC0)])
                for m in range(4):
                    w = [spv] + (first_w if kdg == 0 else [])

                    def b(e, sap=sap, m=m, kdg=kdg):
                        return e.matmul(banks[m], sap[:, ts_(m, P)],
                                        ygv[:, kdg, :], start=(kdg == 0),
                                        stop=(kdg == IC - 1))
                    mk.op("tensor", b, w, s_pe if m == 3 else None, 1)
                st_mo.done((s_pe, s_pe.n))
            mm_pe = (s_pe, s_pe.n)
            for m in range(4):
                act(lambda e, m=m, mh=mh: e.activation(
                    out=mov[:, mh * 4 + m, :], in_=banks[m], func=AF.Copy),
                    waits=[mm_pe] if m == 0 else [])
        mo_sc = (s_sc, s_sc.n)

        src_res = x1S if not p2 else sS
        dst_res = sS if not p2 else out_d
        if not p2:
            mmgroup([ones1], [rows[0:1, 3072:3584]], ps[4][:, 0:512],
                    waits=[b1_dv, (s_dv, s_dv.n), (s_sc, s_sc.n)], inc=s_pe)
            mmgroup([ones1], [rows[0:1, 1536:2048]], ps[5][:, 0:512], inc=s_pe)
            lnb_pe = (s_pe, s_pe.n)
        for m in range(FC):
            xslot, xpv = st_x1.put(
                src_res[seq][ts_(m, P), trg],
                extra=[(s_ac, s_ac.n)] if not p2 else [])
            oslot, opv = st_ow.put(
                outwB[dir_ * D:(dir_ + 1) * D]
                .rearrange("(c p) m -> p c m", p=P)[:, :, ts_(m, P)],
                extra=[(s_pc, PC0)])
            bank, bw = rotB.get()
            w = ([mo_sc, (s_dv, s_dv.n)] if m == 0 else []) + bw + [opv]
            mmgroup([oslot[:, c, :] for c in range(FC)],
                    [mov[:, c, :] for c in range(FC)],
                    bank, waits=w, inc=s_pe)
            pv = (s_pe, s_pe.n)
            j = m % 2
            wts = [pv, xpv, ss_prev[j]]
            if not p2:
                dve(lambda e, bank=bank, j=j, m=m, xslot=xslot:
                    e.scalar_tensor_tensor(
                        out=sstg[:, ts_(j, CH)], in0=bank,
                        scalar=outb[:, m:m + 1], in1=xslot, op0=ALU.add,
                        op1=ALU.add), waits=wts + [lnb_pe])
                dve(lambda e, j=j, xslot=xslot: e.tensor_tensor(
                    out=ytmp, in0=xslot, in1=ps[4][:, 0:512], op=ALU.mult),
                    edge=True)
                dve(lambda e, j=j: e.tensor_tensor(
                    out=sstg[:, ts_(j, CH)], in0=sstg[:, ts_(j, CH)], in1=ytmp,
                    op=ALU.add), edge=True)
                dve(lambda e, j=j: e.tensor_tensor(
                    out=sstg[:, ts_(j, CH)], in0=sstg[:, ts_(j, CH)],
                    in1=ps[5][:, 0:512], op=ALU.add), edge=True)
            else:
                dve(lambda e, bank=bank, j=j, xslot=xslot: e.tensor_tensor(
                    out=sstg[:, ts_(j, CH)], in0=bank, in1=xslot, op=ALU.add),
                    waits=wts)
            rotB.free((s_dv, s_dv.n))
            st_x1.done((s_dv, s_dv.n))
            st_ow.done((s_pe, s_pe.n))
            sdv = (s_dv, s_dv.n)
            mk.dma("sync", dst_res[seq][ts_(m, P), trg], sstg[:, ts_(j, CH)],
                   waits=[sdv], inc=s_st)
            ss_prev[j] = (s_st, s_st.n)

    def pass_setup(dir_):
        for dst, src in ((Dv, d_in[dir_]), (dtb, dtb_in[dir_]),
                         (cbv, cb_in[dir_]), (cwv, cw_in[dir_])):
            mk.dma("sync", dst, src, waits=[(s_dv, s_dv.n), (s_po, s_po.n)],
                   inc=s_cl)
        mk.dma("sync", xproj_sb.rearrange("p (c m) -> p c m", c=IC),
               xprojB[dir_].rearrange("(c p) m -> p c m", p=P),
               waits=[(s_pc, PC0), (s_pe, s_pe.n)], inc=s_cl)
        mk.dma("sync", dtw_sb, dtwB[dir_], waits=[(s_pc, PC0)], inc=s_cl)
        mk.dma("sync", h_out[:, 0:IC * DS], alog[dir_],
               waits=[(s_dv, s_dv.n), (s_po, s_po.n)], inc=s_cl)
        clw = (s_cl, s_cl.n)
        act(lambda e: e.activation(out=Aneg, in_=h_out[:, 0:IC * DS],
                                   func=AF.Exp), waits=[clw], edge=True)
        asc = (s_sc, s_sc.n)
        dve(lambda e: e.tensor_scalar_mul(out=Aneg, in0=Aneg, scalar1=-1.0),
            waits=[asc], edge=True)
        mk.fence()

    pass_setup(0)
    for seq in range(NSEQ):
        for f in range(NSC):
            ffn_phase(False, seq, f)
            mamba_phase(False, seq, 2 * f, first_of_seq=(f == 0))
            mamba_phase(False, seq, 2 * f + 1, first_of_seq=False)
    pass_setup(1)
    for seq in range(NSEQ):
        for f in range(NSC - 1, -1, -1):
            mamba_phase(True, seq, 2 * f + 1, first_of_seq=(f == NSC - 1))
            mamba_phase(True, seq, 2 * f, first_of_seq=False)
            ffn_phase(True, seq, f)

    mk.fence(engines=("sync",))
    mk.op("sync", lambda e: e.nop())
    mk.emit()
    return mk.nc


_NC_CACHE = {}


def kernel(**inputs):
    xs = np.asarray(inputs["L0_attn_x"], np.float32)

    def f32(a):
        return np.ascontiguousarray(np.asarray(a, np.float32))

    w_gate = f32(inputs["L0_ffn_w_gate"])
    w_in = f32(inputs["L0_ffn_w_in"])
    w_out = f32(inputs["L0_ffn_w_out"])
    wgT = np.ascontiguousarray(w_gate.T)
    wiT = np.ascontiguousarray(w_in.transpose(0, 2, 1))
    woT = np.ascontiguousarray(w_out.transpose(0, 2, 1))

    def dirpack(tag):
        inw = f32(inputs["L0_attn_in_w_" + tag])
        xpj = f32(inputs["L0_attn_xproj_w_" + tag])
        dtw = f32(inputs["L0_attn_dt_w_" + tag])
        mout = f32(inputs["L0_attn_mout_w_" + tag])
        alog_ = f32(inputs["L0_attn_A_log_" + tag])
        Dd = f32(inputs["L0_attn_D_" + tag])
        dtbv = f32(inputs["L0_attn_dt_b_" + tag])
        cbb = f32(inputs["L0_attn_conv_b_" + tag])
        cww = f32(inputs["L0_attn_conv_w_" + tag])[:, 0, :]
        return (np.ascontiguousarray(inw.T), np.ascontiguousarray(xpj.T),
                np.ascontiguousarray(dtw.T), np.ascontiguousarray(mout.T),
                np.ascontiguousarray(alog_.reshape(IC, P, DS).transpose(1, 0, 2)
                                     .reshape(P, IC * DS)),
                np.ascontiguousarray(Dd.reshape(IC, P).T),
                np.ascontiguousarray(dtbv.reshape(IC, P).T),
                np.ascontiguousarray(cbb.reshape(IC, P).T),
                np.ascontiguousarray(cww.reshape(IC, P, 4).transpose(1, 0, 2)
                                     .reshape(P, IC * 4)))

    pf = dirpack("f")
    pb = dirpack("b")
    shared = dict(
        wgT=wgT, wiT=wiT, woT=woT,
        inwT=np.stack([pf[0], pb[0]]), xprojT=np.stack([pf[1], pb[1]]),
        dtwT=np.stack([pf[2], pb[2]]), moutT=np.stack([pf[3], pb[3]]),
        alog=np.stack([pf[4], pb[4]]), d_r=np.stack([pf[5], pb[5]]),
        dtb_r=np.stack([pf[6], pb[6]]), cb_r=np.stack([pf[7], pb[7]]),
        cw_r=np.stack([pf[8], pb[8]]),
        outwT=np.ascontiguousarray(f32(inputs["L0_attn_out_w"]).T),
        outb_r=np.ascontiguousarray(
            f32(inputs["L0_attn_out_b"]).reshape(FC, P).T),
        gam_r=np.ascontiguousarray(
            f32(inputs["L0_attn_gamma"]).reshape(FC, P).T),
        bet_r=np.ascontiguousarray(
            f32(inputs["L0_attn_beta"]).reshape(FC, P).T),
        ident=np.eye(P, dtype=np.float32),
        constv=np.broadcast_to(
            np.array([EPS, -0.5, 1.0, -1.0, 0.0, 0.5, 0.0, 0.0], np.float32),
            (P, 8)).copy(),
        ones1=np.ones((1, P), np.float32),
        onesk=np.ones((P, 1), np.float32),
    )

    in_maps = []
    for c in range(NCORES):
        m = dict(shared)
        m["x"] = np.ascontiguousarray(xs[2 * c:2 * c + 2].transpose(0, 2, 1))
        in_maps.append(m)

    if "nc" not in _NC_CACHE:
        _NC_CACHE["nc"] = build_program()
    res = run_bass_kernel_spmd(_NC_CACHE["nc"], in_maps,
                               core_ids=list(range(NCORES)))
    outs = [np.asarray(res.results[c]["out"], np.float32).transpose(0, 2, 1)
            for c in range(NCORES)]
    return np.ascontiguousarray(np.concatenate(outs, axis=0))


if __name__ == "__main__":
    nc = build_program()
    print("built")
